# version 29
# baseline (speedup 1.0000x reference)
import numpy as np
from contextlib import ExitStack

import concourse.bass as bass
import concourse.mybir as mybir
from concourse.bass_utils import run_bass_kernel_spmd
from concourse.aot_env import allow_ici_all_to_all

F32 = mybir.dt.float32
F32R = mybir.dt.float32r
BF16 = mybir.dt.bfloat16
AF = mybir.ActivationFunctionType
ALU = mybir.AluOpType
AX = mybir.AxisListType

NCORES = 8


class Cfg:
    def __init__(self, B=2, LSEQ=8192, D=2048, H=4096, LC=512, NPASS=8):
        self.B, self.LSEQ, self.D, self.H, self.LC, self.NPASS = B, LSEQ, D, H, LC, NPASS
        self.R, self.N, self.K = 128, 16, 4
        self.DC = D // NCORES
        self.G = self.DC // 128
        self.NCH = LSEQ // LC
        self.NCHT = self.B * self.NCH
        self.NTL = (self.B * LSEQ) // 128
        self.TPC = LC // 128
        self.ROWS = B * LSEQ
        self.PR = self.ROWS // NPASS
        self.MT = self.PR // NCORES // 128
        self.CHG = self.NCHT // NPASS
        self.KT1 = D // 128
        self.KTH = H // 128
        self.MB1 = H // 128
        self.MBH = H // 128
        assert self.G >= 1 and self.TPC >= 1 and self.MT >= 1 and LC % 512 == 0


class Sem:
    def __init__(self, es, nc, name):
        self.h = es.enter_context(nc.semaphore(name))
        self.c = 0

    def tick(self, ins, n=1):
        ins.then_inc(self.h, n)
        self.c += n
        return self.c


def _w(eng, *deps):
    for s, v in deps:
        assert v <= s.c, f"wait on future count {v} > emitted {s.c}"
        if v > 0:
            eng.wait_ge(s.h, v)


def _wf(eng, *deps):
    for s, v in deps:
        if v > 0:
            eng.wait_ge(s.h, v)


def build(cfg: Cfg):
    nc = bass.Bass("TRN2", target_bir_lowering=False, debug=False)
    B, LSEQ, D, H, LC = cfg.B, cfg.LSEQ, cfg.D, cfg.H, cfg.LC
    G, N, R, DC, K = cfg.G, cfg.N, cfg.R, cfg.DC, cfg.K
    NCH, NCHT, TPC, NTL = cfg.NCH, cfg.NCHT, cfg.TPC, cfg.NTL
    NPASS, MT, CHG = cfg.NPASS, cfg.MT, cfg.CHG
    KT1, KTH, MB1 = cfg.KT1, cfg.KTH, cfg.MB1
    MBH = cfg.MBH = H // 128
    GL = G * LC
    FT = R + 2 * N
    NLB = LC // 512
    DB = D // 512

    def inp(name, shape, dt=F32):
        return nc.dram_tensor(name, list(shape), dt, kind="ExternalInput").ap()

    xsl = inp("xsl", [B, LSEQ, DC])
    xrows = inp("xrows", [NPASS * MT, 128, D])
    wxpT = inp("wxpT", [DC, FT])
    wdtT = inp("wdtT", [R, DC])
    cw = inp("cw", [128, G * K])
    cb = inp("cb", [128, G])
    dtb = inp("dtb", [128, G])
    alog = inp("alog", [128, G * N])
    dsk = inp("dsk", [128, G])
    NS1 = MB1 // 2
    NS2 = KTH * (H // 512)
    NS3 = KTH * (D // 512)
    w1f = inp("w1f", [NS1, 128, 4096])
    w2f = inp("w2f", [NS2, 128, 512])
    w3f = inp("w3f", [NS3, 128, 512])
    b1r = inp("b1r", [128, MB1])
    b2row = inp("b2row", [1, H])
    b3r = inp("b3r", [1, D])
    ident = inp("ident", [128, 128])

    out = nc.dram_tensor("out", [NPASS * MT, 128, D], F32, kind="ExternalOutput").ap()

    b2bd = nc.dram_tensor("b2bd", [1, H], BF16)
    b3bd = nc.dram_tensor("b3bd", [1, D], BF16)
    wt1b = nc.dram_tensor("wt1b", [NS1, 128, 4096], BF16)
    wt2b = nc.dram_tensor("wt2b", [NS2, 128, 512], BF16)
    wt3b = nc.dram_tensor("wt3b", [NS3, 128, 512], BF16)
    statp = nc.dram_tensor("statp", [128, 2 * NTL], F32)
    statf = nc.dram_tensor("statf", [128, 2 * NTL], F32)
    xdp = nc.dram_tensor("xdp", [NCHT, FT, LC], F32)
    xdf = nc.dram_tensor("xdf", [NCHT, FT, LC], F32)
    ya2a_i = nc.dram_tensor("ya2a_i", [NPASS, NCORES * MT * 128, DC], F32)
    ya2a_o = nc.dram_tensor("ya2a_o", [NPASS, NCORES * MT * 128, DC], F32)
    xmid_sp = nc.dram_tensor("xmid_sp", [2, MT, 128, D], F32)

    rg = [list(range(NCORES))]

    es = ExitStack()
    E = es.enter_context
    sX = Sem(es, nc, "sX")
    sG = Sem(es, nc, "sG")
    sC = Sem(es, nc, "sC")
    sP = Sem(es, nc, "sP")
    sV = Sem(es, nc, "sV")
    sA = Sem(es, nc, "sA")
    sO = Sem(es, nc, "sO")
    sB = Sem(es, nc, "sB")
    sB1 = Sem(es, nc, "sB1")
    sC0 = Sem(es, nc, "sC0")
    sC1 = Sem(es, nc, "sC1")
    sW = Sem(es, nc, "sW")
    sW1 = Sem(es, nc, "sW1")
    sW2 = Sem(es, nc, "sW2")
    sW3 = Sem(es, nc, "sW3")
    sGI0 = Sem(es, nc, "sGI0")
    sGI1 = Sem(es, nc, "sGI1")
    sGO0 = Sem(es, nc, "sGO0")
    sGO1 = Sem(es, nc, "sGO1")
    semGI = [sGI0, sGI1]
    semGO = [sGO0, sGO1]
    semB = [sB, sB1]
    semC = [sC0, sC1]
    semW = [sW, sW1, sW2, sW3]
    bcnt = [0]
    ccnt = [0]
    wcnt = [0]
    CW1, CW2, CW3 = {}, {}, {}

    p_id = E(nc.sbuf_tensor("p_id", [128, 128], F32))
    p_idb = E(nc.sbuf_tensor("p_idb", [128, 128], BF16))
    p_wxpTf = E(nc.sbuf_tensor("p_wxpTf", [128, G * FT], F32))
    p_wxpT = E(nc.sbuf_tensor("p_wxpT", [128, G * FT], F32R))
    p_wdtTf = E(nc.sbuf_tensor("p_wdtTf", [128, DC], F32))
    p_wdtT = E(nc.sbuf_tensor("p_wdtT", [128, DC], F32R))
    p_cw = E(nc.sbuf_tensor("p_cw", [128, G * K], F32))
    p_cb = E(nc.sbuf_tensor("p_cb", [128, G], F32))
    p_dtb = E(nc.sbuf_tensor("p_dtb", [128, G], F32))
    p_A = E(nc.sbuf_tensor("p_A", [128, G * N], F32))
    p_D = E(nc.sbuf_tensor("p_D", [128, G], F32))
    p_b1 = E(nc.sbuf_tensor("p_b1", [128, MB1], F32))
    p_ones = E(nc.sbuf_tensor("p_ones", [1, 128], F32))
    p_onesb = E(nc.sbuf_tensor("p_onesb", [1, 128], BF16))
    p_bs = E(nc.sbuf_tensor("p_bs", [1, 2 * 512], BF16))
    p_st = E(nc.sbuf_tensor("p_st", [128, G * N], F32))
    p_sum = E(nc.sbuf_tensor("p_sum", [128, NTL], F32))
    p_sq = E(nc.sbuf_tensor("p_sq", [128, NTL], F32))
    p_nm = E(nc.sbuf_tensor("p_nm", [128, NTL], F32))
    p_rs = E(nc.sbuf_tensor("p_rs", [128, NTL], F32))
    p_tmp = E(nc.sbuf_tensor("p_tmp", [128, NTL], F32))
    p_eps = E(nc.sbuf_tensor("p_eps", [128, 1], F32))

    pB0 = E(nc.psum_tensor("pB0", [128, 512], F32))
    pB1 = E(nc.psum_tensor("pB1", [128, 512], F32))
    pC0 = E(nc.psum_tensor("pC0", [128, 512], F32))
    pC1 = E(nc.psum_tensor("pC1", [128, 512], F32))
    pF0 = E(nc.psum_tensor("pF0", [128, 512], F32))
    pF1 = E(nc.psum_tensor("pF1", [128, 512], F32))
    pP = E(nc.psum_tensor("pP", [128, 512], F32))
    pT = E(nc.psum_tensor("pT", [128, 512], F32))

    def pTg(g):
        return pT[:, 0:128] if G == 1 else pT[:, 128 * g:128 * (g + 1)]

    with es, nc.Block() as block:
        @block.sync
        def _(sync):
            for dst, src in [
                (p_id[:, :], ident),
                (p_wxpTf[:, :].rearrange("p (g f) -> p g f", g=G),
                 wxpT.rearrange("(g p) f -> p g f", p=128)),
                (p_wdtTf[:, :], wdtT),
                (p_cw[:, :], cw), (p_cb[:, :], cb), (p_dtb[:, :], dtb),
                (p_A[:, :], alog), (p_D[:, :], dsk),
                (p_b1[:, :], b1r),
            ]:
                sX.tick(sync.dma_start(out=dst, in_=src), 16)

        @block.gpsimd
        def _(g):
            sG.tick(g.dma_start(out=b2bd.ap(), in_=b2row), 16)
            sG.tick(g.dma_start(out=b3bd.ap(), in_=b3r), 16)
            for pc in range(4):
                sG.tick(g.dma_start(out=wt1b.ap()[4 * pc:4 * pc + 4],
                                    in_=w1f[4 * pc:4 * pc + 4]), 16)
                CW1[pc] = sG.c
            for pc in range(8):
                sG.tick(g.dma_start(out=wt2b.ap()[32 * pc:32 * pc + 32],
                                    in_=w2f[32 * pc:32 * pc + 32]), 16)
                CW2[pc] = sG.c
            for pc in range(4):
                sG.tick(g.dma_start(out=wt3b.ap()[32 * pc:32 * pc + 32],
                                    in_=w3f[32 * pc:32 * pc + 32]), 16)
                CW3[pc] = sG.c
        CAST_B = 32

        @block.vector
        def _(v):
            _w(v, (sX, sX.c))
            sV.tick(v.memset(p_ones[:, :], 1.0))
            sV.tick(v.memset(p_onesb[:, :], 1.0))
            sV.tick(v.memset(p_eps[:, :], 1e-5))
            sV.tick(v.memset(p_st[:, :], 0.0))
            sV.tick(v.tensor_copy(p_idb[:, :], p_id[:, :]))
            sV.tick(v.tensor_copy(p_wxpT[:, :], p_wxpTf[:, :]))
            sV.tick(v.tensor_copy(p_wdtT[:, :], p_wdtTf[:, :]))

        @block.scalar
        def _(sc):
            _w(sc, (sX, sX.c))
            sA.tick(sc.activation(p_tmp[:, 0:G * N], p_A[:, :], AF.Exp))

        @block.vector
        def _(v):
            _w(v, (sA, sA.c))
            sV.tick(v.tensor_scalar_mul(p_A[:, :], p_tmp[:, 0:G * N], -1.0))

        mes = ExitStack()
        ME = mes.enter_context
        m_x = ME(nc.sbuf_tensor("m_x", [128, 2 * DC], F32))
        m_ln = ME(nc.sbuf_tensor("m_ln", [128, DC], F32))
        m_sq = ME(nc.sbuf_tensor("m_sq", [128, DC], F32))
        m_xbuf = ME(nc.sbuf_tensor("m_xbuf", [128, G * (LC + 3)], F32))
        m_u = ME(nc.sbuf_tensor("m_u", [128, 3 * GL], F32))
        m_ur = ME(nc.sbuf_tensor("m_ur", [128, GL], F32R))
        m_du = ME(nc.sbuf_tensor("m_du", [128, GL], F32))
        m_xpd = ME(nc.sbuf_tensor("m_xpd", [128, LC], F32))
        m_xpbc = ME(nc.sbuf_tensor("m_xpbc", [2 * N, LC], F32))
        m_dtin = ME(nc.sbuf_tensor("m_dtin", [128, 2 * LC], F32))
        m_dtr = ME(nc.sbuf_tensor("m_dtr", [128, 2 * LC], F32R))
        m_ta = ME(nc.sbuf_tensor("m_ta", [128, G * LC], F32))
        m_tb = ME(nc.sbuf_tensor("m_tb", [128, G * LC], F32))
        m_del = ME(nc.sbuf_tensor("m_del", [128, 2 * GL], F32))
        m_dA = ME(nc.sbuf_tensor("m_dA", [128, 2 * GL], F32))
        m_dBu = ME(nc.sbuf_tensor("m_dBu", [128, 2 * GL], F32))
        m_h = ME(nc.sbuf_tensor("m_h", [128, 2 * GL], F32))
        m_t = ME(nc.sbuf_tensor("m_t", [128, 2 * GL], F32))
        m_y = ME(nc.sbuf_tensor("m_y", [128, GL], F32))
        m_Bs = ME(nc.sbuf_tensor("m_Bs", [128, 2 * LC], F32))
        m_Cs = ME(nc.sbuf_tensor("m_Cs", [128, 2 * LC], F32))
        m_yr = ME(nc.sbuf_tensor("m_yr", [128, 2 * DC], F32))

        RW = MT * 128
        f_pool = ME(nc.sbuf_tensor("f_pool", [128, 2 * D], F32))
        f_yt = f_pool[:, 0:D]
        f_xr = f_pool[:, D:2 * D]
        f_h2r = f_pool.ap().bitcast(BF16)
        f_t = ME(nc.sbuf_tensor("f_t", [128, D], F32))
        f_h1 = ME(nc.sbuf_tensor("f_h1", [128, MB1 * RW], BF16))
        f_h2T = f_h1
        f_x2T = ME(nc.sbuf_tensor("f_x2T", [128, KT1 * RW], BF16))
        f_o3b = f_x2T.ap().bitcast(F32)
        f_ws1 = ME(nc.sbuf_tensor("f_ws1", [128, 2 * 4096], BF16))
        f_ws23 = ME(nc.sbuf_tensor("f_ws23", [128, 8 * 512], BF16))
        f_bn = ME(nc.sbuf_tensor("f_bn", [128, 6 * 8], F32))
        f_sg = ME(nc.sbuf_tensor("f_sg", [128, 16], F32))

        with mes:
            for jt in range(NTL):
                b = (jt * 128) // LSEQ
                l0 = (jt * 128) % LSEQ
                par = jt % 2
                xa = m_x[:, par * DC:(par + 1) * DC]

                @block.sync
                def _(sync, xa=xa, b=b, l0=l0, jt=jt):
                    _w(sync, (sV, sV.c), (sA, sA.c))
                    sX.tick(sync.dma_start(out=xa, in_=xsl[b, l0:l0 + 128, :]), 16)

                @block.vector
                def _(v, xa=xa, jt=jt):
                    _w(v, (sX, sX.c))
                    sV.tick(v.reduce_sum(p_sum[:, jt:jt + 1], xa, AX.X))

                @block.scalar
                def _(sc, xa=xa):
                    _w(sc, (sX, sX.c))
                    sA.tick(sc.activation(m_sq[:, :], xa, AF.Square))

                @block.vector
                def _(v, jt=jt):
                    _w(v, (sA, sA.c))
                    sV.tick(v.reduce_sum(p_sq[:, jt:jt + 1], m_sq[:, :], AX.X))

            @block.sync
            def _(sync):
                _w(sync, (sV, sV.c))
                sX.tick(sync.dma_start(out=statp.ap()[:, 0:NTL], in_=p_sum[:, :]), 16)
                sX.tick(sync.dma_start(out=statp.ap()[:, NTL:2 * NTL], in_=p_sq[:, :]), 16)
                _w(sync, (sX, sX.c))

            @block.gpsimd
            def _(g):
                _w(g, (sX, sX.c))
                sC.tick(g.collective_compute(
                    "AllReduce", ALU.add, replica_groups=rg,
                    ins=[statp.ap().opt()], outs=[statf.ap().opt()]), 1)

            @block.sync
            def _(sync):
                _w(sync, (sC, sC.c))
                sX.tick(sync.dma_start(out=p_sum[:, :], in_=statf.ap()[:, 0:NTL]), 16)
                sX.tick(sync.dma_start(out=p_sq[:, :], in_=statf.ap()[:, NTL:2 * NTL]), 16)

            @block.vector
            def _(v):
                _w(v, (sX, sX.c))
                sV.tick(v.tensor_scalar_mul(p_nm[:, :], p_sum[:, :], -1.0 / D))
                _w(v, (sV, sV.c))
                sV.tick(v.tensor_mul(p_tmp[:, :], p_nm[:, :], p_nm[:, :]))
                _w(v, (sV, sV.c))
                sV.tick(v.tensor_scalar(p_rs[:, :], p_sq[:, :], 1.0 / D, None, ALU.mult))
                _w(v, (sV, sV.c))
                sV.tick(v.tensor_sub(p_rs[:, :], p_rs[:, :], p_tmp[:, :]))

            @block.scalar
            def _(sc):
                _w(sc, (sV, sV.c))
                sA.tick(sc.activation(p_tmp[:, :], p_rs[:, :], AF.Ln, bias=p_eps[:, 0:1]))
                _w(sc, (sA, sA.c))
                sA.tick(sc.activation(p_rs[:, :], p_tmp[:, :], AF.Exp, scale=-0.5))
            STAT_V, STAT_A = sV.c, sA.c

            arx_done = {}
            a2a_done = {}
            xpd_dma = {}
            xpj_evac_last = [0]
            xbuf_evac_last = [0]
            lnc, B1tr, xld, xbe = {}, {}, {}, {}
            conv_tick = {}
            ufree = {}
            carry = {"Bev": [0, 0], "ym": [0, 0], "scd": [0, 0], "dbu": 0}
            yfin_prev = [0]


            def emit_B1(t):
                b, tc = t // NCH, t % NCH
                l0 = tc * LC
                par = t % 2
                uoff = (t % 3) * GL

                for j in range(TPC):
                    jt = b * (LSEQ // 128) + (l0 // 128) + j
                    xpar = j % 2
                    xa = m_x[:, xpar * DC:(xpar + 1) * DC]

                    @block.sync
                    def _(sync, xa=xa, b=b, l0=l0, j=j, t=t):
                        _w(sync, (sV, lnc.get((t, j - 2),
                                              lnc.get((t - 1, TPC - 2 + j), 0))))
                        if t < 2:
                            _w(sync, (sV, STAT_V), (sA, STAT_A))
                        sX.tick(sync.dma_start(
                            out=xa, in_=xsl[b, l0 + 128 * j:l0 + 128 * (j + 1), :]), 16)
                        _w(sync, (sX, sX.c))
                        xld[(t, j)] = sX.c

                    @block.vector
                    def _(v, xa=xa, jt=jt, j=j, t=t):
                        _w(v, (sX, xld[(t, j)]), (sA, STAT_A), (sV, STAT_V))
                        _w(v, (sP, B1tr.get((t, j - 1),
                                            B1tr.get((t - 1, TPC - 1), 0))))
                        sV.tick(v.tensor_scalar(m_ln[:, :], xa, p_nm[:, jt:jt + 1],
                                                p_rs[:, jt:jt + 1], ALU.add, ALU.mult))
                        lnc[(t, j)] = sV.c

                    @block.tensor
                    def _(tn, j=j, t=t):
                        _w(tn, (sV, lnc[(t, j)]))
                        if j == 0:
                            _w(tn, (sA, yev.get((t - 3, TPC - 1), 0)))
                        else:
                            _w(tn, (sA, xbe[(t, j - 1)]))
                        for g in range(G):
                            sP.tick(tn.transpose(pTg(g), m_ln[:, 128 * g:128 * (g + 1)],
                                                 p_id[:, :]))
                        B1tr[(t, j)] = sP.c

                    @block.scalar
                    def _(sc, j=j, t=t):
                        _w(sc, (sP, B1tr[(t, j)]))
                        if t >= 1 and j == 0:
                            _w(sc, (sV, conv_tick[t - 1]))
                        for g in range(G):
                            o = g * (LC + 3) + 3 + 128 * j
                            sA.tick(sc.copy(m_xbuf[:, o:o + 128], pTg(g)))
                        xbe[(t, j)] = sA.c
                    xbuf_evac_last[0] = sA.c

                @block.vector
                def _(v, t=t, par=par, uoff=uoff, tc=tc):
                    _w(v, (sA, sA.c))
                    if t == 0:
                        for g in range(G):
                            sV.tick(v.memset(
                                m_xbuf[:, g * (LC + 3):g * (LC + 3) + 3], 0.0))
                        _w(v, (sV, sV.c))
                    if t >= 3:
                        _w(v, (sV, ufree[t - 3]))
                    for g in range(G):
                        xb = m_xbuf[:, g * (LC + 3):(g + 1) * (LC + 3)]
                        ug = m_u[:, uoff + g * LC:uoff + (g + 1) * LC]
                        sV.tick(v.tensor_scalar(ug, xb[:, 0:LC], p_cw[:, g * K:g * K + 1],
                                                p_cb[:, g:g + 1], ALU.mult, ALU.add))
                        for k in range(1, K):
                            _w(v, (sV, sV.c))
                            sV.tick(v.scalar_tensor_tensor(
                                ug, xb[:, k:k + LC], p_cw[:, g * K + k:g * K + k + 1],
                                ug, ALU.mult, ALU.add))
                    for g in range(G):
                        _w(v, (sV, sV.c))
                        xb = m_xbuf[:, g * (LC + 3):(g + 1) * (LC + 3)]
                        if tc == NCH - 1:
                            sV.tick(v.memset(xb[:, 0:3], 0.0))
                        else:
                            sV.tick(v.tensor_copy(xb[:, 0:3], xb[:, LC:LC + 3]))
                    _w(v, (sV, sV.c))
                    sV.tick(v.tensor_copy(m_ur[:, 0:GL], m_u[:, uoff:uoff + GL]))
                conv_tick[t] = sV.c

                @block.tensor
                def _(tn, uoff=uoff):
                    _w(tn, (sV, sV.c), (sA, sA.c))
                    for g in range(G):
                        sP.tick(tn.matmul(pP[:, :], p_wxpT[:, g * FT:g * FT + 128],
                                          m_ur[:, g * LC:(g + 1) * LC],
                                          start=(g == 0), stop=(g == G - 1)))

                @block.scalar
                def _(sc, t=t):
                    _w(sc, (sP, sP.c))
                    if t >= 1:
                        _w(sc, (sX, xpd_dma[t - 1]))
                    sA.tick(sc.copy(m_xpd[:, :], pP[:, :]))

                @block.tensor
                def _(tn, uoff=uoff):
                    _w(tn, (sA, sA.c))
                    for g in range(G):
                        sP.tick(tn.matmul(pP[0:2 * N, :],
                                          p_wxpT[:, g * FT + 128:(g + 1) * FT],
                                          m_ur[:, g * LC:(g + 1) * LC],
                                          start=(g == 0), stop=(g == G - 1)))

                @block.scalar
                def _(sc):
                    _w(sc, (sP, sP.c))
                    sA.tick(sc.copy(m_xpbc[:, :], pP[0:2 * N, :]))
                xpj_evac_last[0] = sA.c

                @block.sync
                def _(sync, t=t):
                    _w(sync, (sA, sA.c))
                    sX.tick(sync.dma_start(out=xdp.ap()[t, 0:128, :], in_=m_xpd[:, :]), 16)
                    sX.tick(sync.dma_start(out=xdp.ap()[t, 128:FT, :], in_=m_xpbc[:, :]), 16)
                    _w(sync, (sX, sX.c))
                xpd_dma[t] = sX.c

            def emit_AR(t):
                @block.gpsimd
                def _(g, t=t):
                    _w(g, (sX, xpd_dma[t]))
                    sC.tick(g.collective_compute(
                        "AllReduce", ALU.add, replica_groups=rg,
                        ins=[xdp.ap()[t].opt()], outs=[xdf.ap()[t].opt()]), 1)
                arx_done[t] = sC.c

            D_rnd, D_dtmm, D_spact, D_du, D_spv = {}, {}, {}, {}, {}
            D_exp, D_scan, D_st, D_dbu = {}, {}, {}, {}
            D_yC, D_add, D_yfin, D_ytrend, D_ydend = {}, {}, {}, {}, {}
            bcB, bcC, ytr, yev, D_ydma = {}, {}, {}, {}, {}

            def pDT(g):
                return pP if g == 0 else pT

            def emit_B2(t):
                b, tc = t // NCH, t % NCH
                cpar = t % 2
                co = cpar * LC
                cog = cpar * GL
                uoff = (t % 3) * GL
                kpass = t // CHG

                @block.gpsimd
                def _(g_, t=t):
                    for tt in (t + 1, t + 2):
                        if tt in xpd_dma and tt not in arx_done:
                            _w(g_, (sX, xpd_dma[tt]))
                            sC.tick(g_.collective_compute(
                                "AllReduce", ALU.add, replica_groups=rg,
                                ins=[xdp.ap()[tt].opt()], outs=[xdf.ap()[tt].opt()]), 1)
                            arx_done[tt] = sC.c

                @block.sync
                def _(sync, t=t, co=co):
                    _w(sync, (sC, arx_done[t]))
                    _w(sync, (sV, D_rnd.get(t - 2, 0)))
                    sX.tick(sync.dma_start(out=m_dtin[:, co:co + LC],
                                           in_=xdf.ap()[t, 0:128, :]), 16)
                    _w(sync, (sX, sX.c))
                xdf_dma = sX.c

                @block.vector
                def _(v, t=t, co=co):
                    _w(v, (sX, xdf_dma), (sP, D_dtmm.get(t - 2, 0)))
                    sV.tick(v.tensor_copy(m_dtr[:, co:co + LC], m_dtin[:, co:co + LC]))
                D_rnd[t] = sV.c

                @block.scalar
                def _(sync, t=t):
                    _w(sync, (sC, arx_done[t]))
                    _w(sync, (sV, D_dbu.get((t - 1, N - 1), 0)))
                    _w(sync, (sV, D_yC.get((t - 1, N - 1), 0)))
                    sbx = semB[bcnt[0] % 2]
                    bcnt[0] += 1
                    sbx.tick(sync.dma_start(
                        out=m_Bs[:, :].rearrange("p (i l) -> p i l", i=2),
                        in_=xdf.ap()[t, 128:130, :].unsqueeze(0)
                            .broadcast_to([128, 2, LC])), 16)
                    bcB[(t, 0)] = bcB[(t, 1)] = (sbx, sbx.c)
                    scx = semC[ccnt[0] % 2]
                    ccnt[0] += 1
                    scx.tick(sync.dma_start(
                        out=m_Cs[:, :].rearrange("p (i l) -> p i l", i=2),
                        in_=xdf.ap()[t, 128 + N:130 + N, :].unsqueeze(0)
                            .broadcast_to([128, 2, LC])), 16)
                    bcC[(t, 0)] = bcC[(t, 1)] = (scx, scx.c)

                @block.tensor
                def _(tn, t=t, co=co):
                    _w(tn, (sV, D_rnd[t]), (sA, xpj_evac_last[0]))
                    for g in range(G):
                        sP.tick(tn.matmul(pDT(g)[:, :], p_wdtT[:, 128 * g:128 * (g + 1)],
                                          m_dtr[:, co:co + LC], start=True, stop=True))
                D_dtmm[t] = sP.c

                @block.scalar
                def _(sc, t=t, cog=cog):
                    _w(sc, (sP, D_dtmm[t]))
                    _w(sc, (sA, D_exp.get((t - 2, N - 1), 0)))
                    _w(sc, (sV, D_du.get(t - 2, 0)))
                    _w(sc, (sV, D_spv.get(t - 1, 0)))
                    for g in range(G):
                        sA.tick(sc.activation(m_ta[:, g * LC:(g + 1) * LC], pDT(g)[:, :],
                                              AF.Abs, bias=p_dtb[:, g:g + 1]))
                    for g in range(G):
                        sA.tick(sc.activation(m_del[:, cog + g * LC:cog + (g + 1) * LC],
                                              pDT(g)[:, :], AF.Relu,
                                              bias=p_dtb[:, g:g + 1]))
                    for g in range(G):
                        sA.tick(sc.activation(m_tb[:, g * LC:(g + 1) * LC],
                                              m_ta[:, g * LC:(g + 1) * LC],
                                              AF.Exp, scale=-1.0))
                    for g in range(G):
                        sA.tick(sc.activation(m_ta[:, g * LC:(g + 1) * LC],
                                              m_tb[:, g * LC:(g + 1) * LC],
                                              AF.Ln, bias=1.0))
                D_spact[t] = sA.c

                @block.vector
                def _(v, t=t, tc=tc, cog=cog, uoff=uoff):
                    _w(v, (sA, D_spact[t]))
                    for g in range(G):
                        dg = m_del[:, cog + g * LC:cog + (g + 1) * LC]
                        sV.tick(v.tensor_add(dg, dg, m_ta[:, g * LC:(g + 1) * LC]))
                    D_spv[t] = sV.c
                    _w(v, (sV, sV.c))
                    sV.tick(v.tensor_mul(m_du[:, 0:GL], m_del[:, cog:cog + GL],
                                         m_u[:, uoff:uoff + GL]))
                    D_du[t] = sV.c
                    _w(v, (sV, sV.c))
                    if tc == 0:
                        sV.tick(v.memset(p_st[:, :], 0.0))
                    for n in (0, 1):
                        npar = n % 2
                        _w(v, bcB[(t, n)])
                        du3 = m_du[:, 0:GL].rearrange("p (g l) -> p g l", g=G)
                        dBu3 = m_dBu[:, npar * GL:(npar + 1) * GL].rearrange(
                            "p (g l) -> p g l", g=G)
                        B3 = m_Bs[:, npar * LC:(npar + 1) * LC].unsqueeze(1).broadcast_to(
                            [128, G, LC])
                        sV.tick(v.tensor_tensor(dBu3, du3, B3, ALU.mult))
                        D_dbu[(t, n)] = sV.c

                @block.scalar
                def _(sc, t=t, cog=cog):
                    for n in (0, 1):
                        npar = n % 2
                        _w(sc, (sV, D_scan.get((t - 1, N - 2 + n), 0)))
                        for g in range(G):
                            sA.tick(sc.activation(
                                m_dA[:, npar * GL + g * LC:npar * GL + (g + 1) * LC],
                                m_del[:, cog + g * LC:cog + (g + 1) * LC], AF.Exp,
                                scale=p_A[:, g * N + n:g * N + n + 1]))
                        D_exp[(t, n)] = sA.c

                for n in range(N):
                    npar = n % 2
                    hoff = npar * GL

                    @block.vector
                    def _(v, t=t, n=n, npar=npar, hoff=hoff, tc=tc):
                        _w(v, (sA, D_exp[(t, n)]))
                        for g in range(G):
                            sV.tick(v.tensor_tensor_scan(
                                m_h[:, hoff + g * LC:hoff + (g + 1) * LC],
                                m_dA[:, hoff + g * LC:hoff + (g + 1) * LC],
                                m_dBu[:, hoff + g * LC:hoff + (g + 1) * LC],
                                p_st[:, g * N + n:g * N + n + 1],
                                ALU.mult, ALU.add))
                        D_scan[(t, n)] = sV.c

                    if n + 1 < N:
                        @block.vector
                        def _(v, t=t, n=n, npar=npar):
                            np1 = (n + 1) % 2
                            _w(v, bcB[(t, n + 1)])
                            du3 = m_du[:, 0:GL].rearrange("p (g l) -> p g l", g=G)
                            dBu3 = m_dBu[:, np1 * GL:(np1 + 1) * GL].rearrange(
                                "p (g l) -> p g l", g=G)
                            B3 = m_Bs[:, np1 * LC:(np1 + 1) * LC].unsqueeze(1).broadcast_to(
                                [128, G, LC])
                            sV.tick(v.tensor_tensor(dBu3, du3, B3, ALU.mult))
                            D_dbu[(t, n + 1)] = sV.c

                    if n + 2 < N:
                        @block.scalar
                        def _(sc, t=t, n=n, npar=npar, cog=cog):
                            _w(sc, (sV, D_scan[(t, n)]))
                            for g in range(G):
                                sA.tick(sc.activation(
                                    m_dA[:, npar * GL + g * LC:npar * GL + (g + 1) * LC],
                                    m_del[:, cog + g * LC:cog + (g + 1) * LC], AF.Exp,
                                    scale=p_A[:, g * N + n + 2:g * N + n + 3]))
                            D_exp[(t, n + 2)] = sA.c

                    @block.vector
                    def _(v, t=t, n=n, npar=npar, hoff=hoff):
                        if n == N - 1:
                            _w(v, (sV, sV.c))
                        hlast = bass.AP(m_h, hoff + LC - 1, [[2 * GL, 128], [LC, G], [1, 1]])
                        stv = bass.AP(p_st, n, [[G * N, 128], [N, G], [1, 1]])
                        sV.tick(v.tensor_copy(stv, hlast))
                        _w(v, bcC[(t, n)])
                        h3 = m_h[:, hoff:hoff + GL].rearrange("p (g l) -> p g l", g=G)
                        t3 = m_t[:, hoff:hoff + GL].rearrange("p (g l) -> p g l", g=G)
                        C3 = m_Cs[:, npar * LC:(npar + 1) * LC].unsqueeze(1).broadcast_to(
                            [128, G, LC])
                        sV.tick(v.tensor_tensor(t3, h3, C3, ALU.mult))
                        D_yC[(t, n)] = sV.c
                        _w(v, (sV, sV.c))
                        if n == 0:
                            _w(v, (sP, D_ytrend.get(t - 1, 0)))
                            sV.tick(v.tensor_copy(m_y[:, :], m_t[:, hoff:hoff + GL]))
                        else:
                            sV.tick(v.tensor_add(m_y[:, :], m_y[:, :],
                                                 m_t[:, hoff:hoff + GL]))
                        D_add[(t, n)] = sV.c

                    if n + 2 < N:
                        @block.scalar
                        def _(sync, t=t, n=n, npar=npar):
                            if n % 2 == 0:
                                _w(sync, (sV, D_dbu[(t, n + 1)]))
                                sbx = semB[bcnt[0] % 2]
                                bcnt[0] += 1
                                sbx.tick(sync.dma_start(
                                    out=m_Bs[:, :].rearrange("p (i l) -> p i l", i=2),
                                    in_=xdf.ap()[t, 130 + n:132 + n, :].unsqueeze(0)
                                        .broadcast_to([128, 2, LC])), 16)
                                bcB[(t, n + 2)] = bcB[(t, n + 3)] = (sbx, sbx.c)
                            _w(sync, (sV, D_yC[(t, n)]))
                            scx = semC[ccnt[0] % 2]
                            ccnt[0] += 1
                            scx.tick(sync.dma_start(
                                out=m_Cs[:, npar * LC:(npar + 1) * LC],
                                in_=xdf.ap()[t, 128 + N + n + 2:129 + N + n + 2, :]
                                    .broadcast_to([128, LC])), 16)
                            bcC[(t, n + 2)] = (scx, scx.c)

                @block.vector
                def _(v, t=t, uoff=uoff):
                    _w(v, (sV, D_add[(t, N - 1)]))
                    for g in range(G):
                        sV.tick(v.scalar_tensor_tensor(
                            m_y[:, g * LC:(g + 1) * LC],
                            m_u[:, uoff + g * LC:uoff + (g + 1) * LC],
                            p_D[:, g:g + 1],
                            m_y[:, g * LC:(g + 1) * LC], ALU.mult, ALU.add))
                D_yfin[t] = sV.c
                ufree[t] = sV.c

            def emit_B2_tail(t):
                kpass = t // CHG
                for j in range(TPC):
                    jp = j % 2

                    @block.tensor
                    def _(tn, t=t, j=j):
                        if j == 0:
                            _w(tn, (sV, D_yfin[t]), (sA, D_spact[t]))
                            _w(tn, (sA, xbuf_evac_last[0]))
                        else:
                            _w(tn, (sA, yev[(t, j - 1)]))
                        for g in range(G):
                            ins = tn.transpose(
                                pTg(g), m_y[:, g * LC + 128 * j:g * LC + 128 * (j + 1)],
                                p_id[:, :])
                        sP.tick(ins)
                        ytr[(t, j)] = sP.c

                    @block.scalar
                    def _(sc, t=t, j=j, jp=jp):
                        _w(sc, (sP, ytr[(t, j)]))
                        if j < 2:
                            _w(sc, (sX, D_ydma.get((t - 1, j + 2), 0)))
                        else:
                            _w(sc, (sX, D_ydma[(t, j - 2)]))
                        for g in range(G):
                            sA.tick(sc.copy(m_yr[:, jp * DC + 128 * g:jp * DC + 128 * (g + 1)],
                                            pTg(g)))
                        yev[(t, j)] = sA.c

                    @block.sync
                    def _(sync, t=t, j=j, jp=jp, kpass=kpass):
                        _w(sync, (sA, yev[(t, j)]))
                        Jt = (t % CHG) * TPC + j
                        shard, slot = Jt % NCORES, Jt // NCORES
                        r0 = shard * (MT * 128) + slot * 128
                        sX.tick(sync.dma_start(
                            out=ya2a_i.ap()[kpass, r0:r0 + 128, :],
                            in_=m_yr[:, jp * DC:(jp + 1) * DC]), 16)
                        _w(sync, (sX, sX.c))
                        D_ydma[(t, j)] = sX.c
                D_ytrend[t] = ytr[(t, TPC - 1)]
                D_ydend[t] = D_ydma[(t, TPC - 1)]

                if (t + 1) % CHG == 0:
                    @block.gpsimd
                    def _(g_, t=t, kpass=kpass):
                        _w(g_, (sX, D_ydend[t]))
                        with allow_ici_all_to_all():
                            sC.tick(g_.collective_compute(
                                "AllToAll", ALU.bypass, replica_groups=rg,
                                ins=[ya2a_i.ap()[kpass].opt()],
                                outs=[ya2a_o.ap()[kpass].opt()]), 1)
                        a2a_done[kpass] = sC.c

            HS = H // 512
            DS = D // 512
            pW = [pB0, pB1, pC0, pC1]
            pFT = pF0
            fst = {}

            def emit_ln_rowmajor(srcap):
                @block.vector
                def _(v):
                    _w(v, (sV, sV.c))
                    for i in range(DB):
                        sV.tick(v.bn_stats(f_bn[:, 6 * i:6 * (i + 1)],
                                           srcap[:, 512 * i:512 * (i + 1)]))
                    _w(v, (sV, sV.c))
                    sV.tick(v.bn_aggr(f_sg[:, 0:2], f_bn[:, 0:6 * DB]))

                @block.scalar
                def _(sc):
                    _w(sc, (sV, sV.c))
                    sA.tick(sc.activation(f_sg[:, 4:5], f_sg[:, 1:2], AF.Ln,
                                          bias=p_eps[:, 0:1]))
                    _w(sc, (sA, sA.c))
                    sA.tick(sc.activation(f_sg[:, 3:4], f_sg[:, 4:5], AF.Exp, scale=-0.5))

                @block.vector
                def _(v):
                    _w(v, (sA, sA.c), (sV, sV.c))
                    sV.tick(v.tensor_scalar_mul(f_sg[:, 2:3], f_sg[:, 0:1], -1.0))
                    _w(v, (sV, sV.c))
                    sV.tick(v.tensor_mul(f_sg[:, 5:6], f_sg[:, 2:3], f_sg[:, 3:4]))

            def emit_prep(kp, m):
                s = fst.setdefault(kp, {})

                @block.sync
                def _(sync, kp=kp, m=m, s=s):
                    _w(sync, (sC, a2a_done[kp]))
                    _w(sync, (sP, fst.get(kp - 1, {}).get('h2T_end', 0)))
                    _w(sync, (sV, fst.get(kp - 1, {}).get('o3v1', 0)))
                    if m == 1:
                        _w(sync, (sV, s['pool_free0']))
                    for i in range(NCORES):
                        r0 = i * RW + m * 128
                        sX.tick(sync.dma_start(out=f_yt[:, i * DC:(i + 1) * DC],
                                               in_=ya2a_o.ap()[kp, r0:r0 + 128, :]), 16)
                    sX.tick(sync.dma_start(out=f_xr[:, :], in_=xrows[kp * MT + m]), 16)
                    _w(sync, (sX, sX.c))
                yt_dma = sX.c

                @block.vector
                def _(v, yt_dma=yt_dma):
                    _w(v, (sX, yt_dma))
                emit_ln_rowmajor(f_yt)

                @block.vector
                def _(v, kp=kp, m=m):
                    _w(v, (sV, sV.c))
                    if m == 0:
                        _w(v, (sX, fst.get(kp - 1, {}).get('out_dma', 0)))
                    sV.tick(v.tensor_scalar(f_t[:, :], f_yt[:, :], f_sg[:, 3:4],
                                            f_sg[:, 5:6], ALU.mult, ALU.add))
                    _w(v, (sV, sV.c))
                    sV.tick(v.tensor_add(f_yt[:, :], f_xr[:, :], f_t[:, :]))
                xm_tick = sV.c

                @block.sync
                def _(sync, kp=kp, m=m, xm_tick=xm_tick, s=s):
                    _w(sync, (sV, xm_tick))
                    _w(sync, (sX, fst.get(kp - 2, {}).get('xm_ld', 0)))
                    sX.tick(sync.dma_start(out=xmid_sp.ap()[kp % 2, m], in_=f_yt[:, :]), 16)
                    _w(sync, (sX, sX.c))
                    s['spill%d' % m] = sX.c

                emit_ln_rowmajor(f_yt)

                @block.vector
                def _(v, m=m):
                    _w(v, (sV, sV.c))
                    f_x2 = f_t.ap().bitcast(BF16)[:, 0:D]
                    sV.tick(v.tensor_scalar(f_x2, f_yt[:, :], f_sg[:, 3:4],
                                            f_sg[:, 5:6], ALU.mult, ALU.add))
                x2_tick = sV.c
                s['pool_free%d' % m] = x2_tick

                for kt2 in range(KT1 // 4):
                    @block.tensor
                    def _(tn, m=m, kt2=kt2, x2_tick=x2_tick):
                        _w(tn, (sV, x2_tick), (sA, sA.c))
                        for q in range(4):
                            kt = kt2 * 4 + q
                            ins = tn.transpose(
                                pFT.ap().bitcast(BF16)[:, 256 * q:256 * q + 128],
                                f_t.ap().bitcast(BF16)[:, 128 * kt:128 * (kt + 1)],
                                p_idb[:, :])
                        sP.tick(ins)
                    tr4 = sP.c

                    @block.scalar
                    def _(sc, kp=kp, m=m, kt2=kt2, tr4=tr4):
                        _w(sc, (sP, tr4))
                        if m == 0 and kt2 == 0:
                            _w(sc, (sX, fst.get(kp - 1, {}).get('out_dma', 0)))
                        dst = bass.AP(f_x2T, kt2 * 4 * RW + 128 * m,
                                      [[KT1 * RW, 128], [RW, 4], [1, 128]])
                        src = pFT.ap().bitcast(BF16)[:, 0:1024].rearrange(
                            "p (q j) -> p q j", q=4)[:, :, 0:128]
                        sA.tick(sc.copy(dst, src))
                s['x2Ttr%d' % m] = sP.c
                s['x2T%d' % m] = sA.c

            def emit_W1(kp, half):
                s = fst[kp]
                evt = s.setdefault('w1evt', {})
                sl = s.setdefault('w1slab', {})
                for mg in range(half * (NS1 // 2), (half + 1) * (NS1 // 2)):
                    wpar = mg % 2

                    @block.sync
                    def _(sync, kp=kp, mg=mg, wpar=wpar, s=s):
                        _w(sync, (sP, s.get('w1used%d' % (mg - 2), 0)))
                        if mg < 2:
                            _w(sync, (sP, fst.get(kp - 1, {}).get('w1end', 0)))
                        if kp == 0:
                            _w(sync, (sG, CW1[mg // 4]))
                        swx = semW[wcnt[0] % 4]
                        wcnt[0] += 1
                        swx.tick(sync.dma_start(
                            out=f_ws1[:, wpar * 4096:(wpar + 1) * 4096],
                            in_=wt1b.ap()[mg]), 16)
                        sl[mg] = (swx, swx.c)

                    for mb in range(2):
                        mi = mg * 2 + mb
                        pF = pW[mi % 2]

                        @block.tensor
                        def _(tn, kp=kp, mg=mg, mb=mb, mi=mi, pF=pF, wpar=wpar, s=s):
                            _w(tn, sl[mg])
                            _w(tn, (sA, max(s['x2T0'], s['x2T1'], evt.get(mi - 2, 0))))
                            if mi == 0:
                                _w(tn, (sV, fst.get(kp - 1, {}).get('w3stats', 0)))
                                _w(tn, (sA, fst.get(kp - 1, {}).get('w3evac', 0)))
                            for kt in range(KT1):
                                sP.tick(tn.matmul(
                                    pF[:, 0:RW],
                                    f_ws1[:, wpar * 4096 + (kt * 2 + mb) * 128:
                                          wpar * 4096 + (kt * 2 + mb + 1) * 128],
                                    f_x2T[:, kt * RW:(kt + 1) * RW],
                                    start=(kt == 0), stop=(kt == KT1 - 1),
                                    skip_group_check=True))
                        mm_done = sP.c

                        @block.scalar
                        def _(sc, kp=kp, mi=mi, pF=pF, mm_done=mm_done):
                            _w(sc, (sP, mm_done))
                            if mi == 0:
                                _w(sc, (sP, fst.get(kp - 1, {}).get('w3mm_end', 0)))
                            sA.tick(sc.activation(f_h1[:, mi * RW:(mi + 1) * RW],
                                                  pF[:, 0:RW], AF.Relu,
                                                  bias=p_b1[:, mi:mi + 1]))
                            evt[mi] = sA.c
                    s['w1used%d' % mg] = sP.c
                if half == 1:
                    s['w1end'] = sP.c
                    s['w1evt_end'] = sA.c

            def emit_W2(kp, hp):
                s = fst[kp]
                BT = 2
                slab, mmb = {}, {}
                for bt in range(KTH // BT):
                    @block.sync
                    def _(sync, kp=kp, hp=hp, bt=bt, s=s):
                        _w(sync, (sP, mmb.get(bt - 2, 0)))
                        if bt < 2:
                            _w(sync, (sP, s.get('w2mm%d' % (hp - 1), 0)))
                            if hp == 0:
                                _w(sync, (sP, fst.get(kp - 1, {}).get('w3mm_end', 0)))
                        if kp == 0:
                            _w(sync, (sG, CW2[(hp * 64 + bt * 4) // 32]))
                        s0 = hp * 64 + bt * 4
                        swx = semW[wcnt[0] % 4]
                        wcnt[0] += 1
                        swx.tick(sync.dma_start(
                            out=f_ws23[:, (bt % 2) * 2048:(bt % 2) * 2048 + 2048]
                                .rearrange("p (i j) -> p i j", i=4),
                            in_=wt2b.ap()[s0:s0 + 4].rearrange("i p j -> p i j")), 16)
                        slab[bt] = (swx, swx.c)

                    @block.tensor
                    def _(tn, kp=kp, hp=hp, bt=bt, s=s):
                        _w(tn, slab[bt])
                        if bt == 0:
                            _w(tn, (sA, s['w1evt_end']))
                            _w(tn, (sA, s.get('w2evac%d' % (hp - 1), 0)))
                        for kt in range(bt * BT, (bt + 1) * BT):
                            for m in range(2):
                                for sx2 in range(2):
                                    slot = (kt % (2 * BT)) * 2 + sx2
                                    sP.tick(tn.matmul(
                                        pW[2 * m + sx2][:, 0:512],
                                        f_h1[:, kt * RW + m * 128:kt * RW + (m + 1) * 128],
                                        f_ws23[:, slot * 512:(slot + 1) * 512],
                                        start=(kt == 0), stop=False,
                                        skip_group_check=True))
                        mmb[bt] = sP.c

                @block.sync
                def _(sync, kp=kp, hp=hp, s=s):
                    _w(sync, (sP, s.get('bmm%d' % (hp - 1), 0)))
                    if kp == 0 and hp == 0:
                        _w(sync, (sG, CAST_B))
                    swx = semW[wcnt[0] % 4]
                    wcnt[0] += 1
                    swx.tick(sync.dma_start(
                        out=p_bs[0:1, :], in_=b2bd.ap()[0:1, hp * 1024:(hp + 1) * 1024]), 16)
                    s['bld%d' % hp] = (swx, swx.c)

                @block.tensor
                def _(tn, kp=kp, hp=hp, s=s):
                    _w(tn, s['bld%d' % hp])
                    for m in range(2):
                        for sx2 in range(2):
                            sP.tick(tn.matmul(pW[2 * m + sx2][:, 0:512], p_onesb[:, :],
                                              p_bs[0:1, sx2 * 512:(sx2 + 1) * 512],
                                              start=False, stop=True,
                                              skip_group_check=True))
                s['w2mm%d' % hp] = sP.c
                s['bmm%d' % hp] = sP.c

                @block.scalar
                def _(sc, kp=kp, hp=hp, s=s):
                    _w(sc, (sP, s['w2mm%d' % hp]))
                    if hp == 0:
                        _w(sc, (sV, s['pool_free1']))
                        _w(sc, (sX, s['spill1']))
                    for m in range(2):
                        for sx2 in range(2):
                            hs = hp * 2 + sx2
                            sA.tick(sc.activation(
                                f_h2r[:, m * 4096 + hs * 512:m * 4096 + (hs + 1) * 512],
                                pW[2 * m + sx2][:, 0:512], AF.Relu))
                s['w2evac%d' % hp] = sA.c

            def emit_h2T(kp):
                s = fst[kp]
                for m in range(2):
                    for q4 in range(8):
                        @block.tensor
                        def _(tn, kp=kp, m=m, q4=q4, s=s):
                            _w(tn, (sA, sA.c))
                            if m == 0 and q4 == 0:
                                _w(tn, (sA, s['w2evac3']))
                            for q in range(4):
                                ht = q4 * 4 + q
                                ins = tn.transpose(
                                    pFT.ap().bitcast(BF16)[:, 256 * q:256 * q + 128],
                                    f_h2r[:, m * 4096 + ht * 128:m * 4096 + (ht + 1) * 128],
                                    p_idb[:, :])
                            sP.tick(ins)
                        tr = sP.c

                        @block.scalar
                        def _(sc, kp=kp, m=m, q4=q4, tr=tr, s=s):
                            _w(sc, (sP, tr))
                            if m == 0 and q4 == 0:
                                _w(sc, (sP, s['w2mm3']))
                            dst = bass.AP(f_h1, q4 * 4 * RW + m * 128,
                                          [[MB1 * RW, 128], [RW, 4], [1, 128]])
                            src = pFT.ap().bitcast(BF16)[:, 0:1024].rearrange(
                                "p (q j) -> p q j", q=4)[:, :, 0:128]
                            sA.tick(sc.copy(dst, src))
                s['h2T_end'] = sP.c
                s['h2Tev_end'] = sA.c

            def emit_W3(kp):
                s = fst[kp]
                BT3 = 2
                slab, mmb = {}, {}
                for dsp in range(2):
                    for bt in range(KTH // BT3):
                        fb = dsp * (KTH // BT3) + bt

                        @block.sync
                        def _(sync, kp=kp, dsp=dsp, bt=bt, fb=fb, s=s):
                            _w(sync, (sP, mmb.get(fb - 2, 0)))
                            if fb < 2:
                                _w(sync, (sP, s['w2mm3']))
                            if kp == 0:
                                _w(sync, (sG, CW3[(dsp * 64 + bt * 4) // 32]))
                            s0 = dsp * 64 + bt * 4
                            swx = semW[wcnt[0] % 4]
                            wcnt[0] += 1
                            swx.tick(sync.dma_start(
                                out=f_ws23[:, (fb % 2) * 2048:(fb % 2) * 2048 + 2048]
                                    .rearrange("p (i j) -> p i j", i=4),
                                in_=wt3b.ap()[s0:s0 + 4].rearrange("i p j -> p i j")), 16)
                            slab[(dsp, bt)] = (swx, swx.c)

                        @block.tensor
                        def _(tn, kp=kp, dsp=dsp, bt=bt, fb=fb, s=s):
                            _w(tn, slab[(dsp, bt)])
                            if bt == 0:
                                _w(tn, (sA, s['h2Tev_end']))
                                if dsp == 0:
                                    _w(tn, (sA, s['w2evac3']))
                                else:
                                    _w(tn, (sA, s['w3evac0']))
                            for kt in range(bt * BT3, (bt + 1) * BT3):
                                for m in range(2):
                                    for sx2 in range(2):
                                        slot = (kt % (2 * BT3)) * 2 + sx2
                                        sP.tick(tn.matmul(
                                            pW[2 * m + sx2][:, 0:512],
                                            f_h2T[:, kt * RW + m * 128:kt * RW + (m + 1) * 128],
                                            f_ws23[:, slot * 512:(slot + 1) * 512],
                                            start=(kt == 0), stop=False,
                                            skip_group_check=True))
                            mmb[fb] = sP.c

                    @block.sync
                    def _(sync, kp=kp, dsp=dsp, s=s):
                        _w(sync, (sP, s['bmm3'] if dsp == 0 else s['w3mm0']))
                        swx = semW[wcnt[0] % 4]
                        wcnt[0] += 1
                        swx.tick(sync.dma_start(
                            out=p_bs[0:1, :],
                            in_=b3bd.ap()[0:1, dsp * 1024:(dsp + 1) * 1024]), 16)
                        s['b3ld%d' % dsp] = (swx, swx.c)

                    @block.tensor
                    def _(tn, kp=kp, dsp=dsp, s=s):
                        _w(tn, s['b3ld%d' % dsp])
                        for m in range(2):
                            for sx2 in range(2):
                                sP.tick(tn.matmul(pW[2 * m + sx2][:, 0:512], p_onesb[:, :],
                                                  p_bs[0:1, sx2 * 512:(sx2 + 1) * 512],
                                                  start=False, stop=True,
                                                  skip_group_check=True))
                    s['w3mm%d' % dsp] = sP.c

                    @block.scalar
                    def _(sc, kp=kp, dsp=dsp, s=s):
                        _w(sc, (sP, s['w3mm%d' % dsp]))
                        if dsp == 0:
                            _w(sc, (sP, s['x2Ttr1']))
                            _w(sc, (sP, s['w1end']))
                        for m in range(2):
                            stage = f_t if m == 0 else f_o3b
                            for sx2 in range(2):
                                ds = dsp * 2 + sx2
                                sA.tick(sc.copy(stage[:, ds * 512:(ds + 1) * 512],
                                                pW[2 * m + sx2][:, 0:512]))
                    s['w3evac%d' % dsp] = sA.c
                s['w3mm_end'] = s['w3mm1']
                s['w3evac'] = s['w3evac1']

            def emit_W3_fin(kp):
                s = fst[kp]

                @block.vector
                def _(v, kp=kp, s=s):
                    _w(v, (sA, s['w3evac1']))
                    for m in range(2):
                        stage = f_t if m == 0 else f_o3b
                        for ds in range(4):
                            sl8 = m * 4 + ds
                            sV.tick(v.bn_stats(f_bn[:, 6 * sl8:6 * (sl8 + 1)],
                                               stage[:, ds * 512:(ds + 1) * 512]))
                    _w(v, (sV, sV.c))
                    for m in range(2):
                        sV.tick(v.bn_aggr(f_sg[:, 8 * m:8 * m + 2],
                                          f_bn[:, 24 * m:24 * m + 24]))
                s['w3stats1'] = sV.c
                s['w3stats'] = sV.c
                ag_v = sV.c

                @block.scalar
                def _(sc, kp=kp, ag_v=ag_v, s=s):
                    _w(sc, (sA, s['w3evac1']), (sV, ag_v))
                    for m in range(2):
                        o = 8 * m
                        sA.tick(sc.activation(f_sg[:, o + 4:o + 5], f_sg[:, o + 1:o + 2],
                                              AF.Ln, bias=p_eps[:, 0:1]))
                        _w(sc, (sA, sA.c))
                        sA.tick(sc.activation(f_sg[:, o + 3:o + 4], f_sg[:, o + 4:o + 5],
                                              AF.Exp, scale=-0.5))
                ln_a = sA.c

                @block.sync
                def _(sync, kp=kp, s=s):
                    _w(sync, (sP, s['h2T_end']))
                    sX.tick(sync.dma_start(out=f_yt[:, :], in_=xmid_sp.ap()[kp % 2, 0]), 16)
                    sX.tick(sync.dma_start(out=f_xr[:, :], in_=xmid_sp.ap()[kp % 2, 1]), 16)
                    _w(sync, (sX, sX.c))
                s['xm_ld'] = sX.c

                @block.vector
                def _(v, kp=kp, ln_a=ln_a, s=s):
                    _w(v, (sA, ln_a))
                    for m in range(2):
                        o = 8 * m
                        sV.tick(v.tensor_scalar_mul(f_sg[:, o + 2:o + 3],
                                                    f_sg[:, o:o + 1], -1.0))
                        _w(v, (sV, sV.c))
                        sV.tick(v.tensor_mul(f_sg[:, o + 5:o + 6], f_sg[:, o + 2:o + 3],
                                             f_sg[:, o + 3:o + 4]))
                    _w(v, (sX, s['xm_ld']))
                    for m in range(2):
                        o = 8 * m
                        stage = f_t if m == 0 else f_o3b
                        xm = f_yt if m == 0 else f_xr
                        _w(v, (sV, sV.c))
                        sV.tick(v.tensor_scalar(stage[:, 0:D], stage[:, 0:D],
                                                f_sg[:, o + 3:o + 4], f_sg[:, o + 5:o + 6],
                                                ALU.mult, ALU.add))
                        _w(v, (sV, sV.c))
                        sV.tick(v.tensor_add(stage[:, 0:D], stage[:, 0:D], xm[:, :]))
                        if m == 0:
                            s['o3v0'] = sV.c
                    s['o3v1'] = sV.c

                @block.sync
                def _(sync, kp=kp, s=s):
                    _w(sync, (sV, s['o3v0']))
                    sX.tick(sync.dma_start(out=out[kp * MT + 0], in_=f_t[:, 0:D]), 16)
                    _w(sync, (sV, s['o3v1']))
                    sX.tick(sync.dma_start(out=out[kp * MT + 1], in_=f_o3b[:, 0:D]), 16)
                    _w(sync, (sX, sX.c))
                s['out_dma'] = sX.c

            pending = []
            emit_B1(0)
            emit_B1(1)
            emit_AR(0)
            for t in range(NCHT):
                if t + 2 < NCHT:
                    emit_B1(t + 2)
                emit_B2(t)
                if pending:
                    pending.pop(0)()
                emit_B2_tail(t)
                if (t + 1) % CHG == 0:
                    kp = t // CHG
                    pending += [
                        lambda kp=kp: ((emit_W3_fin(kp - 1) if kp > 0 else None),
                                       emit_prep(kp, 0), emit_prep(kp, 1),
                                       emit_W1(kp, 0)),
                        lambda kp=kp: (emit_W1(kp, 1), emit_W2(kp, 0)),
                        lambda kp=kp: (emit_W2(kp, 1), emit_W2(kp, 2)),
                        lambda kp=kp: (emit_W2(kp, 3), emit_h2T(kp), emit_W3(kp)),
                    ]
            while pending:
                pending.pop(0)()
            emit_W3_fin(NPASS - 1)

            @block.sync
            def _(sync):
                _w(sync, (sX, sX.c))

    return nc



def _host_inputs(cfg: Cfg, x, conv_w, conv_b, xproj_w, dtproj_w, dtproj_b,
                 A_log, D_skip, W1, b1, W2, b2, W3, b3):
    B, LSEQ, D, H = cfg.B, cfg.LSEQ, cfg.D, cfg.H
    DC, G, N, K = cfg.DC, cfg.G, cfg.N, cfg.K
    NPASS, MT = cfg.NPASS, cfg.MT
    f32 = np.float32
    C = np.ascontiguousarray

    ident = np.eye(128, dtype=f32)
    xf = x.reshape(B * LSEQ, D)
    w1t = W1.T
    w2t = W2.T
    w3t = W3.T
    KT1, KTH, MB1 = cfg.KT1, cfg.KTH, cfg.MB1
    w1s = C(w1t.reshape(KT1, 128, MB1 // 2, 256).transpose(2, 1, 0, 3)
            .reshape(MB1 // 2, 128, KT1 * 256).astype(f32))
    w2s = C(w2t.reshape(KTH, 128, H // 1024, 2, 512).transpose(2, 0, 3, 1, 4)
            .reshape(KTH * (H // 512), 128, 512).astype(f32))
    w3s = C(w3t.reshape(KTH, 128, D // 1024, 2, 512).transpose(2, 0, 3, 1, 4)
            .reshape(KTH * (D // 512), 128, 512).astype(f32))

    def perm_g(a):
        t = a.reshape(G, 128, -1).transpose(1, 0, 2).reshape(128, -1)
        return C(t.astype(f32))

    in_maps = []
    for c in range(NCORES):
        sl = slice(DC * c, DC * (c + 1))
        PRc = cfg.PR
        tiles = []
        for kp in range(NPASS):
            for m in range(MT):
                r0 = PRc * kp + 128 * (c + NCORES * m)
                tiles.append(xf[r0:r0 + 128])
        xrows = np.stack(tiles, axis=0)

        d = {
            "xsl": C(x[:, :, sl]),
            "xrows": C(xrows.astype(f32)),
            "wxpT": C(xproj_w[:, sl].T.astype(f32)),
            "wdtT": C(dtproj_w[sl, :].T.astype(f32)),
            "cw": perm_g(conv_w[sl, 0, :]),
            "cb": perm_g(conv_b[sl]),
            "dtb": perm_g(dtproj_b[sl]),
            "alog": perm_g(A_log[sl]),
            "dsk": perm_g(D_skip[sl]),
            "w1f": w1s,
            "w2f": w2s,
            "w3f": w3s,
            "b1r": C(b1.reshape(cfg.MB1, 128).T.astype(f32)),
            "b2row": C(b2.reshape(1, cfg.H).astype(f32)),
            "b3r": C(b3.reshape(1, D).astype(f32)),
            "ident": ident,
        }
        in_maps.append(d)
    return in_maps


def _assemble(cfg: Cfg, results):
    B, LSEQ, D = cfg.B, cfg.LSEQ, cfg.D
    out = np.empty((B * LSEQ, D), dtype=np.float32)
    for c in range(NCORES):
        o = np.asarray(results[c]["out"])
        i = 0
        for kp in range(cfg.NPASS):
            for m in range(cfg.MT):
                r0 = cfg.PR * kp + 128 * (c + NCORES * m)
                out[r0:r0 + 128] = o[i]
                i += 1
    return out.reshape(B, LSEQ, D)


_CFG = Cfg()


def kernel(L0_attn_x, L0_attn_conv_w, L0_attn_conv_b, L0_attn_xproj_w,
           L0_attn_dtproj_w, L0_attn_dtproj_b, L0_attn_A_log, L0_attn_D_skip,
           L0_ffn_W1, L0_ffn_b1, L0_ffn_W2, L0_ffn_b2, L0_ffn_W3, L0_ffn_b3):
    cfg = _CFG
    args = [np.asarray(a, dtype=np.float32) for a in
            (L0_attn_x, L0_attn_conv_w, L0_attn_conv_b, L0_attn_xproj_w,
             L0_attn_dtproj_w, L0_attn_dtproj_b, L0_attn_A_log, L0_attn_D_skip,
             L0_ffn_W1, L0_ffn_b1, L0_ffn_W2, L0_ffn_b2, L0_ffn_W3, L0_ffn_b3)]
    in_maps = _host_inputs(cfg, *args)
    nc = build(cfg)
    res = run_bass_kernel_spmd(nc, in_maps, core_ids=list(range(NCORES)))
    return _assemble(cfg, res.results)


# version 30
# speedup vs baseline: 1.1223x; 1.1223x over previous
import numpy as np
from contextlib import ExitStack

import concourse.bass as bass
import concourse.mybir as mybir
from concourse.bass_utils import run_bass_kernel_spmd
from concourse.aot_env import allow_ici_all_to_all

F32 = mybir.dt.float32
F32R = mybir.dt.float32r
BF16 = mybir.dt.bfloat16
AF = mybir.ActivationFunctionType
ALU = mybir.AluOpType
AX = mybir.AxisListType

NCORES = 8


class Cfg:
    def __init__(self, B=2, LSEQ=8192, D=2048, H=4096, LC=512, NPASS=8):
        self.B, self.LSEQ, self.D, self.H, self.LC, self.NPASS = B, LSEQ, D, H, LC, NPASS
        self.R, self.N, self.K = 128, 16, 4
        self.DC = D // NCORES
        self.G = self.DC // 128
        self.NCH = LSEQ // LC
        self.NCHT = self.B * self.NCH
        self.NTL = (self.B * LSEQ) // 128
        self.TPC = LC // 128
        self.ROWS = B * LSEQ
        self.PR = self.ROWS // NPASS
        self.MT = self.PR // NCORES // 128
        self.CHG = self.NCHT // NPASS
        self.KT1 = D // 128
        self.KTH = H // 128
        self.MB1 = H // 128
        self.MBH = H // 128
        assert self.G >= 1 and self.TPC >= 1 and self.MT >= 1 and LC % 512 == 0


class Sem:
    def __init__(self, es, nc, name):
        self.h = es.enter_context(nc.semaphore(name))
        self.c = 0

    def tick(self, ins, n=1):
        ins.then_inc(self.h, n)
        self.c += n
        return self.c


def _w(eng, *deps):
    for s, v in deps:
        assert v <= s.c, f"wait on future count {v} > emitted {s.c}"
        if v > 0:
            eng.wait_ge(s.h, v)


def _wf(eng, *deps):
    for s, v in deps:
        if v > 0:
            eng.wait_ge(s.h, v)


def build(cfg: Cfg):
    nc = bass.Bass("TRN2", target_bir_lowering=False, debug=False)
    B, LSEQ, D, H, LC = cfg.B, cfg.LSEQ, cfg.D, cfg.H, cfg.LC
    G, N, R, DC, K = cfg.G, cfg.N, cfg.R, cfg.DC, cfg.K
    NCH, NCHT, TPC, NTL = cfg.NCH, cfg.NCHT, cfg.TPC, cfg.NTL
    NPASS, MT, CHG = cfg.NPASS, cfg.MT, cfg.CHG
    KT1, KTH, MB1 = cfg.KT1, cfg.KTH, cfg.MB1
    MBH = cfg.MBH = H // 128
    GL = G * LC
    FT = R + 2 * N
    NLB = LC // 512
    DB = D // 512

    def inp(name, shape, dt=F32):
        return nc.dram_tensor(name, list(shape), dt, kind="ExternalInput").ap()

    xsl = inp("xsl", [B, LSEQ, DC])
    xrows = inp("xrows", [NPASS * MT, 128, D])
    wxpT = inp("wxpT", [DC, FT])
    wdtT = inp("wdtT", [R, DC])
    cw = inp("cw", [128, G * K])
    cb = inp("cb", [128, G])
    dtb = inp("dtb", [128, G])
    alog = inp("alog", [128, G * N])
    dsk = inp("dsk", [128, G])
    NS1 = MB1 // 2
    NS2 = KTH * (H // 512)
    NS3 = KTH * (D // 512)
    w1f = inp("w1f", [NS1, 128, 4096])
    w2f = inp("w2f", [NS2, 128, 512])
    w3f = inp("w3f", [NS3, 128, 512])
    b1r = inp("b1r", [128, MB1])
    b2row = inp("b2row", [1, H])
    b3r = inp("b3r", [1, D])
    ident = inp("ident", [128, 128])

    out = nc.dram_tensor("out", [NPASS * MT, 128, D], F32, kind="ExternalOutput").ap()

    b2bd = nc.dram_tensor("b2bd", [1, H], BF16)
    b3bd = nc.dram_tensor("b3bd", [1, D], BF16)
    wt1b = nc.dram_tensor("wt1b", [NS1, 128, 4096], BF16)
    wt2b = nc.dram_tensor("wt2b", [NS2, 128, 512], BF16)
    wt3b = nc.dram_tensor("wt3b", [NS3, 128, 512], BF16)
    statp = nc.dram_tensor("statp", [128, 2 * NTL], F32)
    statf = nc.dram_tensor("statf", [128, 2 * NTL], F32)
    xdp = nc.dram_tensor("xdp", [NCHT, FT, LC], F32)
    xdf = nc.dram_tensor("xdf", [NCHT, FT, LC], F32)
    ya2a_i = nc.dram_tensor("ya2a_i", [NPASS, NCORES * MT * 128, DC], F32)
    ya2a_o = nc.dram_tensor("ya2a_o", [NPASS, NCORES * MT * 128, DC], F32)
    xmid_sp = nc.dram_tensor("xmid_sp", [2, MT, 128, D], F32)

    rg = [list(range(NCORES))]

    es = ExitStack()
    E = es.enter_context
    sX = Sem(es, nc, "sX")
    sG = Sem(es, nc, "sG")
    sC = Sem(es, nc, "sC")
    sP = Sem(es, nc, "sP")
    sV = Sem(es, nc, "sV")
    sA = Sem(es, nc, "sA")
    sO = Sem(es, nc, "sO")
    sB = Sem(es, nc, "sB")
    sB1 = Sem(es, nc, "sB1")
    sC0 = Sem(es, nc, "sC0")
    sC1 = Sem(es, nc, "sC1")
    sW = Sem(es, nc, "sW")
    sW1 = Sem(es, nc, "sW1")
    sW2 = Sem(es, nc, "sW2")
    sW3 = Sem(es, nc, "sW3")
    sGI0 = Sem(es, nc, "sGI0")
    sGI1 = Sem(es, nc, "sGI1")
    sGO0 = Sem(es, nc, "sGO0")
    sGO1 = Sem(es, nc, "sGO1")
    semGI = [sGI0, sGI1]
    semGO = [sGO0, sGO1]
    semB = [sB, sB1]
    semC = [sC0, sC1]
    semW = [sW, sW1, sW2, sW3]
    bcnt = [0]
    ccnt = [0]
    wcnt = [0]
    CW1, CW2, CW3 = {}, {}, {}

    p_id = E(nc.sbuf_tensor("p_id", [128, 128], F32))
    p_idb = E(nc.sbuf_tensor("p_idb", [128, 128], BF16))
    p_wxpTf = E(nc.sbuf_tensor("p_wxpTf", [128, G * FT], F32))
    p_wxpT = E(nc.sbuf_tensor("p_wxpT", [128, G * FT], F32R))
    p_wdtTf = E(nc.sbuf_tensor("p_wdtTf", [128, DC], F32))
    p_wdtT = E(nc.sbuf_tensor("p_wdtT", [128, DC], F32R))
    p_cw = E(nc.sbuf_tensor("p_cw", [128, G * K], F32))
    p_cb = E(nc.sbuf_tensor("p_cb", [128, G], F32))
    p_dtb = E(nc.sbuf_tensor("p_dtb", [128, G], F32))
    p_A = E(nc.sbuf_tensor("p_A", [128, G * N], F32))
    p_D = E(nc.sbuf_tensor("p_D", [128, G], F32))
    p_b1 = E(nc.sbuf_tensor("p_b1", [128, MB1], F32))
    p_ones = E(nc.sbuf_tensor("p_ones", [1, 128], F32))
    p_onesb = E(nc.sbuf_tensor("p_onesb", [1, 128], BF16))
    p_bs = E(nc.sbuf_tensor("p_bs", [1, 2 * 512], BF16))
    p_st = E(nc.sbuf_tensor("p_st", [128, G * N], F32))
    p_sum = E(nc.sbuf_tensor("p_sum", [128, NTL], F32))
    p_sq = E(nc.sbuf_tensor("p_sq", [128, NTL], F32))
    p_nm = E(nc.sbuf_tensor("p_nm", [128, NTL], F32))
    p_rs = E(nc.sbuf_tensor("p_rs", [128, NTL], F32))
    p_tmp = E(nc.sbuf_tensor("p_tmp", [128, NTL], F32))
    p_eps = E(nc.sbuf_tensor("p_eps", [128, 1], F32))

    pB0 = E(nc.psum_tensor("pB0", [128, 512], F32))
    pB1 = E(nc.psum_tensor("pB1", [128, 512], F32))
    pC0 = E(nc.psum_tensor("pC0", [128, 512], F32))
    pC1 = E(nc.psum_tensor("pC1", [128, 512], F32))
    pF0 = E(nc.psum_tensor("pF0", [128, 512], F32))
    pF1 = E(nc.psum_tensor("pF1", [128, 512], F32))
    pP = E(nc.psum_tensor("pP", [128, 512], F32))
    pT = E(nc.psum_tensor("pT", [128, 512], F32))

    def pTg(g):
        return pT[:, 0:128] if G == 1 else pT[:, 128 * g:128 * (g + 1)]

    with es, nc.Block() as block:
        @block.sync
        def _(sync):
            for dst, src in [
                (p_id[:, :], ident),
                (p_wxpTf[:, :].rearrange("p (g f) -> p g f", g=G),
                 wxpT.rearrange("(g p) f -> p g f", p=128)),
                (p_wdtTf[:, :], wdtT),
                (p_cw[:, :], cw), (p_cb[:, :], cb), (p_dtb[:, :], dtb),
                (p_A[:, :], alog), (p_D[:, :], dsk),
                (p_b1[:, :], b1r),
            ]:
                sX.tick(sync.dma_start(out=dst, in_=src), 16)

        @block.gpsimd
        def _(g):
            sG.tick(g.dma_start(out=b2bd.ap(), in_=b2row), 16)
            sG.tick(g.dma_start(out=b3bd.ap(), in_=b3r), 16)
            for pc in range(4):
                sG.tick(g.dma_start(out=wt1b.ap()[4 * pc:4 * pc + 4],
                                    in_=w1f[4 * pc:4 * pc + 4]), 16)
                CW1[pc] = sG.c
            for pc in range(8):
                sG.tick(g.dma_start(out=wt2b.ap()[32 * pc:32 * pc + 32],
                                    in_=w2f[32 * pc:32 * pc + 32]), 16)
                CW2[pc] = sG.c
            for pc in range(4):
                sG.tick(g.dma_start(out=wt3b.ap()[32 * pc:32 * pc + 32],
                                    in_=w3f[32 * pc:32 * pc + 32]), 16)
                CW3[pc] = sG.c
        CAST_B = 32

        @block.vector
        def _(v):
            _w(v, (sX, sX.c))
            sV.tick(v.memset(p_ones[:, :], 1.0))
            sV.tick(v.memset(p_onesb[:, :], 1.0))
            sV.tick(v.memset(p_eps[:, :], 1e-5))
            sV.tick(v.memset(p_st[:, :], 0.0))
            sV.tick(v.tensor_copy(p_idb[:, :], p_id[:, :]))
            sV.tick(v.tensor_copy(p_wxpT[:, :], p_wxpTf[:, :]))
            sV.tick(v.tensor_copy(p_wdtT[:, :], p_wdtTf[:, :]))

        @block.scalar
        def _(sc):
            _w(sc, (sX, sX.c))
            sA.tick(sc.activation(p_tmp[:, 0:G * N], p_A[:, :], AF.Exp))

        @block.vector
        def _(v):
            _w(v, (sA, sA.c))
            sV.tick(v.tensor_scalar_mul(p_A[:, :], p_tmp[:, 0:G * N], -1.0))

        mes = ExitStack()
        ME = mes.enter_context
        m_x = ME(nc.sbuf_tensor("m_x", [128, 2 * DC], F32))
        m_ln = ME(nc.sbuf_tensor("m_ln", [128, DC], F32))
        m_sq = ME(nc.sbuf_tensor("m_sq", [128, DC], F32))
        m_xbuf = ME(nc.sbuf_tensor("m_xbuf", [128, G * (LC + 3)], F32))
        m_u = ME(nc.sbuf_tensor("m_u", [128, 3 * GL], F32))
        m_ur = ME(nc.sbuf_tensor("m_ur", [128, GL], F32R))
        m_du = ME(nc.sbuf_tensor("m_du", [128, GL], F32))
        m_xpd = ME(nc.sbuf_tensor("m_xpd", [128, LC], F32))
        m_xpbc = ME(nc.sbuf_tensor("m_xpbc", [2 * N, LC], F32))
        m_dtin = ME(nc.sbuf_tensor("m_dtin", [128, 2 * LC], F32))
        m_dtr = ME(nc.sbuf_tensor("m_dtr", [128, 2 * LC], F32R))
        m_ta = ME(nc.sbuf_tensor("m_ta", [128, G * LC], F32))
        m_tb = ME(nc.sbuf_tensor("m_tb", [128, G * LC], F32))
        m_del = ME(nc.sbuf_tensor("m_del", [128, 2 * GL], F32))
        m_dA = ME(nc.sbuf_tensor("m_dA", [128, 2 * GL], F32))
        m_dBu = ME(nc.sbuf_tensor("m_dBu", [128, 2 * GL], F32))
        m_h = ME(nc.sbuf_tensor("m_h", [128, 2 * GL], F32))
        m_t = ME(nc.sbuf_tensor("m_t", [128, 2 * GL], F32))
        m_y = ME(nc.sbuf_tensor("m_y", [128, GL], F32))
        m_Bs = ME(nc.sbuf_tensor("m_Bs", [128, 2 * LC], F32))
        m_Cs = ME(nc.sbuf_tensor("m_Cs", [128, 2 * LC], F32))
        m_yr = ME(nc.sbuf_tensor("m_yr", [128, 2 * DC], F32))

        RW = MT * 128
        f_pool = ME(nc.sbuf_tensor("f_pool", [128, 2 * D], F32))
        f_yt = f_pool[:, 0:D]
        f_xr = f_pool[:, D:2 * D]
        f_h2r = f_pool.ap().bitcast(BF16)
        f_t = ME(nc.sbuf_tensor("f_t", [128, D], F32))
        f_h1 = ME(nc.sbuf_tensor("f_h1", [128, MB1 * RW], BF16))
        f_h2T = f_h1
        f_x2T = ME(nc.sbuf_tensor("f_x2T", [128, KT1 * RW], BF16))
        f_o3b = f_x2T.ap().bitcast(F32)
        f_ws1 = ME(nc.sbuf_tensor("f_ws1", [128, 2 * 4096], BF16))
        f_ws23 = ME(nc.sbuf_tensor("f_ws23", [128, 8 * 512], BF16))
        f_bn = ME(nc.sbuf_tensor("f_bn", [128, 6 * 8], F32))
        f_sg = ME(nc.sbuf_tensor("f_sg", [128, 16], F32))

        with mes:
            for jt in range(NTL):
                b = (jt * 128) // LSEQ
                l0 = (jt * 128) % LSEQ
                par = jt % 2
                xa = m_x[:, par * DC:(par + 1) * DC]

                @block.sync
                def _(sync, xa=xa, b=b, l0=l0, jt=jt):
                    _w(sync, (sV, sV.c), (sA, sA.c))
                    sX.tick(sync.dma_start(out=xa, in_=xsl[b, l0:l0 + 128, :]), 16)

                @block.vector
                def _(v, xa=xa, jt=jt):
                    _w(v, (sX, sX.c))
                    sV.tick(v.reduce_sum(p_sum[:, jt:jt + 1], xa, AX.X))

                @block.scalar
                def _(sc, xa=xa):
                    _w(sc, (sX, sX.c))
                    sA.tick(sc.activation(m_sq[:, :], xa, AF.Square))

                @block.vector
                def _(v, jt=jt):
                    _w(v, (sA, sA.c))
                    sV.tick(v.reduce_sum(p_sq[:, jt:jt + 1], m_sq[:, :], AX.X))

            @block.sync
            def _(sync):
                _w(sync, (sV, sV.c))
                sX.tick(sync.dma_start(out=statp.ap()[:, 0:NTL], in_=p_sum[:, :]), 16)
                sX.tick(sync.dma_start(out=statp.ap()[:, NTL:2 * NTL], in_=p_sq[:, :]), 16)
                _w(sync, (sX, sX.c))

            @block.gpsimd
            def _(g):
                _w(g, (sX, sX.c))
                sC.tick(g.collective_compute(
                    "AllReduce", ALU.add, replica_groups=rg,
                    ins=[statp.ap().opt()], outs=[statf.ap().opt()]), 1)

            @block.sync
            def _(sync):
                _w(sync, (sC, sC.c))
                sX.tick(sync.dma_start(out=p_sum[:, :], in_=statf.ap()[:, 0:NTL]), 16)
                sX.tick(sync.dma_start(out=p_sq[:, :], in_=statf.ap()[:, NTL:2 * NTL]), 16)

            @block.vector
            def _(v):
                _w(v, (sX, sX.c))
                sV.tick(v.tensor_scalar_mul(p_nm[:, :], p_sum[:, :], -1.0 / D))
                _w(v, (sV, sV.c))
                sV.tick(v.tensor_mul(p_tmp[:, :], p_nm[:, :], p_nm[:, :]))
                _w(v, (sV, sV.c))
                sV.tick(v.tensor_scalar(p_rs[:, :], p_sq[:, :], 1.0 / D, None, ALU.mult))
                _w(v, (sV, sV.c))
                sV.tick(v.tensor_sub(p_rs[:, :], p_rs[:, :], p_tmp[:, :]))

            @block.scalar
            def _(sc):
                _w(sc, (sV, sV.c))
                sA.tick(sc.activation(p_tmp[:, :], p_rs[:, :], AF.Ln, bias=p_eps[:, 0:1]))
                _w(sc, (sA, sA.c))
                sA.tick(sc.activation(p_rs[:, :], p_tmp[:, :], AF.Exp, scale=-0.5))
            STAT_V, STAT_A = sV.c, sA.c

            arx_done = {}
            a2a_done = {}
            xpd_dma = {}
            xpj_evac_last = [0]
            xbuf_evac_last = [0]
            lnc, B1tr, xld, xbe = {}, {}, {}, {}
            conv_tick = {}
            ufree = {}
            carry = {"Bev": [0, 0], "ym": [0, 0], "scd": [0, 0], "dbu": 0}
            yfin_prev = [0]


            def emit_B1(t):
                b, tc = t // NCH, t % NCH
                l0 = tc * LC
                par = t % 2
                uoff = (t % 3) * GL

                for j in range(TPC):
                    jt = b * (LSEQ // 128) + (l0 // 128) + j
                    xpar = j % 2
                    xa = m_x[:, xpar * DC:(xpar + 1) * DC]

                    @block.sync
                    def _(sync, xa=xa, b=b, l0=l0, j=j, t=t):
                        _w(sync, (sV, lnc.get((t, j - 2),
                                              lnc.get((t - 1, TPC - 2 + j), 0))))
                        if t < 2:
                            _w(sync, (sV, STAT_V), (sA, STAT_A))
                        sX.tick(sync.dma_start(
                            out=xa, in_=xsl[b, l0 + 128 * j:l0 + 128 * (j + 1), :]), 16)
                        _w(sync, (sX, sX.c))
                        xld[(t, j)] = sX.c

                    @block.vector
                    def _(v, xa=xa, jt=jt, j=j, t=t):
                        _w(v, (sX, xld[(t, j)]), (sA, STAT_A), (sV, STAT_V))
                        _w(v, (sP, B1tr.get((t, j - 1),
                                            B1tr.get((t - 1, TPC - 1), 0))))
                        sV.tick(v.tensor_scalar(m_ln[:, :], xa, p_nm[:, jt:jt + 1],
                                                p_rs[:, jt:jt + 1], ALU.add, ALU.mult))
                        lnc[(t, j)] = sV.c

                    @block.tensor
                    def _(tn, j=j, t=t):
                        _w(tn, (sV, lnc[(t, j)]))
                        if j == 0:
                            _w(tn, (sA, yev.get((t - 3, TPC - 1), 0)))
                        else:
                            _w(tn, (sA, xbe[(t, j - 1)]))
                        for g in range(G):
                            sP.tick(tn.transpose(pTg(g), m_ln[:, 128 * g:128 * (g + 1)],
                                                 p_id[:, :]))
                        B1tr[(t, j)] = sP.c

                    @block.scalar
                    def _(sc, j=j, t=t):
                        _w(sc, (sP, B1tr[(t, j)]))
                        if t >= 1 and j == 0:
                            _w(sc, (sV, conv_tick[t - 1]))
                        for g in range(G):
                            o = g * (LC + 3) + 3 + 128 * j
                            sA.tick(sc.copy(m_xbuf[:, o:o + 128], pTg(g)))
                        xbe[(t, j)] = sA.c
                    xbuf_evac_last[0] = sA.c

                @block.vector
                def _(v, t=t, par=par, uoff=uoff, tc=tc):
                    _w(v, (sA, sA.c))
                    if t == 0:
                        for g in range(G):
                            sV.tick(v.memset(
                                m_xbuf[:, g * (LC + 3):g * (LC + 3) + 3], 0.0))
                        _w(v, (sV, sV.c))
                    if t >= 3:
                        _w(v, (sV, ufree[t - 3]))
                    for g in range(G):
                        xb = m_xbuf[:, g * (LC + 3):(g + 1) * (LC + 3)]
                        ug = m_u[:, uoff + g * LC:uoff + (g + 1) * LC]
                        sV.tick(v.tensor_scalar(ug, xb[:, 0:LC], p_cw[:, g * K:g * K + 1],
                                                p_cb[:, g:g + 1], ALU.mult, ALU.add))
                        for k in range(1, K):
                            _w(v, (sV, sV.c))
                            sV.tick(v.scalar_tensor_tensor(
                                ug, xb[:, k:k + LC], p_cw[:, g * K + k:g * K + k + 1],
                                ug, ALU.mult, ALU.add))
                    for g in range(G):
                        _w(v, (sV, sV.c))
                        xb = m_xbuf[:, g * (LC + 3):(g + 1) * (LC + 3)]
                        if tc == NCH - 1:
                            sV.tick(v.memset(xb[:, 0:3], 0.0))
                        else:
                            sV.tick(v.tensor_copy(xb[:, 0:3], xb[:, LC:LC + 3]))
                    _w(v, (sV, sV.c))
                    sV.tick(v.tensor_copy(m_ur[:, 0:GL], m_u[:, uoff:uoff + GL]))
                conv_tick[t] = sV.c

                @block.tensor
                def _(tn, uoff=uoff):
                    _w(tn, (sV, sV.c), (sA, sA.c))
                    for g in range(G):
                        sP.tick(tn.matmul(pP[:, :], p_wxpT[:, g * FT:g * FT + 128],
                                          m_ur[:, g * LC:(g + 1) * LC],
                                          start=(g == 0), stop=(g == G - 1)))

                @block.scalar
                def _(sc, t=t):
                    _w(sc, (sP, sP.c))
                    if t >= 1:
                        _w(sc, (sX, xpd_dma[t - 1]))
                    sA.tick(sc.copy(m_xpd[:, :], pP[:, :]))

                @block.tensor
                def _(tn, uoff=uoff):
                    _w(tn, (sA, sA.c))
                    for g in range(G):
                        sP.tick(tn.matmul(pP[0:2 * N, :],
                                          p_wxpT[:, g * FT + 128:(g + 1) * FT],
                                          m_ur[:, g * LC:(g + 1) * LC],
                                          start=(g == 0), stop=(g == G - 1)))

                @block.scalar
                def _(sc):
                    _w(sc, (sP, sP.c))
                    sA.tick(sc.copy(m_xpbc[:, :], pP[0:2 * N, :]))
                xpj_evac_last[0] = sA.c

                @block.sync
                def _(sync, t=t):
                    _w(sync, (sA, sA.c))
                    sX.tick(sync.dma_start(out=xdp.ap()[t, 0:128, :], in_=m_xpd[:, :]), 16)
                    sX.tick(sync.dma_start(out=xdp.ap()[t, 128:FT, :], in_=m_xpbc[:, :]), 16)
                    _w(sync, (sX, sX.c))
                xpd_dma[t] = sX.c

            def emit_AR(t):
                @block.gpsimd
                def _(g, t=t):
                    _w(g, (sX, max(xpd_dma[t], xpd_dma[t + 1])))
                    sC.tick(g.collective_compute(
                        "AllReduce", ALU.add, replica_groups=rg,
                        ins=[xdp.ap()[t:t + 2].opt()], outs=[xdf.ap()[t:t + 2].opt()]), 1)
                arx_done[t] = arx_done[t + 1] = sC.c

            D_rnd, D_dtmm, D_spact, D_du, D_spv = {}, {}, {}, {}, {}
            D_exp, D_scan, D_st, D_dbu = {}, {}, {}, {}
            D_yC, D_add, D_yfin, D_ytrend, D_ydend = {}, {}, {}, {}, {}
            bcB, bcC, ytr, yev, D_ydma = {}, {}, {}, {}, {}

            def pDT(g):
                return pP if g == 0 else pT

            def emit_B2(t):
                b, tc = t // NCH, t % NCH
                cpar = t % 2
                co = cpar * LC
                cog = cpar * GL
                uoff = (t % 3) * GL
                kpass = t // CHG

                @block.gpsimd
                def _(g_, t=t):
                    for tt in (t + 1, t + 2):
                        if (tt % 2 == 0 and tt in xpd_dma and tt + 1 in xpd_dma
                                and tt not in arx_done):
                            _w(g_, (sX, max(xpd_dma[tt], xpd_dma[tt + 1])))
                            sC.tick(g_.collective_compute(
                                "AllReduce", ALU.add, replica_groups=rg,
                                ins=[xdp.ap()[tt:tt + 2].opt()],
                                outs=[xdf.ap()[tt:tt + 2].opt()]), 1)
                            arx_done[tt] = arx_done[tt + 1] = sC.c

                @block.sync
                def _(sync, t=t, co=co):
                    _w(sync, (sC, arx_done[t]))
                    _w(sync, (sV, D_rnd.get(t - 2, 0)))
                    sX.tick(sync.dma_start(out=m_dtin[:, co:co + LC],
                                           in_=xdf.ap()[t, 0:128, :]), 16)
                    _w(sync, (sX, sX.c))
                xdf_dma = sX.c

                @block.vector
                def _(v, t=t, co=co):
                    _w(v, (sX, xdf_dma), (sP, D_dtmm.get(t - 2, 0)))
                    sV.tick(v.tensor_copy(m_dtr[:, co:co + LC], m_dtin[:, co:co + LC]))
                D_rnd[t] = sV.c

                @block.scalar
                def _(sync, t=t):
                    _w(sync, (sC, arx_done[t]))
                    _w(sync, (sV, D_dbu.get((t - 1, N - 1), 0)))
                    _w(sync, (sV, D_yC.get((t - 1, N - 1), 0)))
                    sbx = semB[bcnt[0] % 2]
                    bcnt[0] += 1
                    sbx.tick(sync.dma_start(
                        out=m_Bs[:, :].rearrange("p (i l) -> p i l", i=2),
                        in_=xdf.ap()[t, 128:130, :].unsqueeze(0)
                            .broadcast_to([128, 2, LC])), 16)
                    bcB[(t, 0)] = bcB[(t, 1)] = (sbx, sbx.c)
                    scx = semC[ccnt[0] % 2]
                    ccnt[0] += 1
                    scx.tick(sync.dma_start(
                        out=m_Cs[:, :].rearrange("p (i l) -> p i l", i=2),
                        in_=xdf.ap()[t, 128 + N:130 + N, :].unsqueeze(0)
                            .broadcast_to([128, 2, LC])), 16)
                    bcC[(t, 0)] = bcC[(t, 1)] = (scx, scx.c)

                @block.tensor
                def _(tn, t=t, co=co):
                    _w(tn, (sV, D_rnd[t]), (sA, xpj_evac_last[0]))
                    for g in range(G):
                        sP.tick(tn.matmul(pDT(g)[:, :], p_wdtT[:, 128 * g:128 * (g + 1)],
                                          m_dtr[:, co:co + LC], start=True, stop=True))
                D_dtmm[t] = sP.c

                @block.scalar
                def _(sc, t=t, cog=cog):
                    _w(sc, (sP, D_dtmm[t]))
                    _w(sc, (sA, D_exp.get((t - 2, N - 1), 0)))
                    _w(sc, (sV, D_du.get(t - 2, 0)))
                    _w(sc, (sV, D_spv.get(t - 1, 0)))
                    for g in range(G):
                        sA.tick(sc.activation(m_ta[:, g * LC:(g + 1) * LC], pDT(g)[:, :],
                                              AF.Abs, bias=p_dtb[:, g:g + 1]))
                    for g in range(G):
                        sA.tick(sc.activation(m_del[:, cog + g * LC:cog + (g + 1) * LC],
                                              pDT(g)[:, :], AF.Relu,
                                              bias=p_dtb[:, g:g + 1]))
                    for g in range(G):
                        sA.tick(sc.activation(m_tb[:, g * LC:(g + 1) * LC],
                                              m_ta[:, g * LC:(g + 1) * LC],
                                              AF.Exp, scale=-1.0))
                    for g in range(G):
                        sA.tick(sc.activation(m_ta[:, g * LC:(g + 1) * LC],
                                              m_tb[:, g * LC:(g + 1) * LC],
                                              AF.Ln, bias=1.0))
                D_spact[t] = sA.c

                @block.vector
                def _(v, t=t, tc=tc, cog=cog, uoff=uoff):
                    _w(v, (sA, D_spact[t]))
                    for g in range(G):
                        dg = m_del[:, cog + g * LC:cog + (g + 1) * LC]
                        sV.tick(v.tensor_add(dg, dg, m_ta[:, g * LC:(g + 1) * LC]))
                    D_spv[t] = sV.c
                    _w(v, (sV, sV.c))
                    sV.tick(v.tensor_mul(m_du[:, 0:GL], m_del[:, cog:cog + GL],
                                         m_u[:, uoff:uoff + GL]))
                    D_du[t] = sV.c
                    _w(v, (sV, sV.c))
                    if tc == 0:
                        sV.tick(v.memset(p_st[:, :], 0.0))
                    for n in (0, 1):
                        npar = n % 2
                        _w(v, bcB[(t, n)])
                        du3 = m_du[:, 0:GL].rearrange("p (g l) -> p g l", g=G)
                        dBu3 = m_dBu[:, npar * GL:(npar + 1) * GL].rearrange(
                            "p (g l) -> p g l", g=G)
                        B3 = m_Bs[:, npar * LC:(npar + 1) * LC].unsqueeze(1).broadcast_to(
                            [128, G, LC])
                        sV.tick(v.tensor_tensor(dBu3, du3, B3, ALU.mult))
                        D_dbu[(t, n)] = sV.c

                @block.scalar
                def _(sc, t=t, cog=cog):
                    for n in (0, 1):
                        npar = n % 2
                        _w(sc, (sV, D_scan.get((t - 1, N - 2 + n), 0)))
                        for g in range(G):
                            sA.tick(sc.activation(
                                m_dA[:, npar * GL + g * LC:npar * GL + (g + 1) * LC],
                                m_del[:, cog + g * LC:cog + (g + 1) * LC], AF.Exp,
                                scale=p_A[:, g * N + n:g * N + n + 1]))
                        D_exp[(t, n)] = sA.c

                for n in range(N):
                    npar = n % 2
                    hoff = npar * GL

                    @block.vector
                    def _(v, t=t, n=n, npar=npar, hoff=hoff, tc=tc):
                        _w(v, (sA, D_exp[(t, n)]))
                        for g in range(G):
                            sV.tick(v.tensor_tensor_scan(
                                m_h[:, hoff + g * LC:hoff + (g + 1) * LC],
                                m_dA[:, hoff + g * LC:hoff + (g + 1) * LC],
                                m_dBu[:, hoff + g * LC:hoff + (g + 1) * LC],
                                p_st[:, g * N + n:g * N + n + 1],
                                ALU.mult, ALU.add))
                        D_scan[(t, n)] = sV.c

                    if n + 1 < N:
                        @block.vector
                        def _(v, t=t, n=n, npar=npar):
                            np1 = (n + 1) % 2
                            _w(v, bcB[(t, n + 1)])
                            du3 = m_du[:, 0:GL].rearrange("p (g l) -> p g l", g=G)
                            dBu3 = m_dBu[:, np1 * GL:(np1 + 1) * GL].rearrange(
                                "p (g l) -> p g l", g=G)
                            B3 = m_Bs[:, np1 * LC:(np1 + 1) * LC].unsqueeze(1).broadcast_to(
                                [128, G, LC])
                            sV.tick(v.tensor_tensor(dBu3, du3, B3, ALU.mult))
                            D_dbu[(t, n + 1)] = sV.c

                    if n + 2 < N:
                        @block.scalar
                        def _(sc, t=t, n=n, npar=npar, cog=cog):
                            _w(sc, (sV, D_scan[(t, n)]))
                            for g in range(G):
                                sA.tick(sc.activation(
                                    m_dA[:, npar * GL + g * LC:npar * GL + (g + 1) * LC],
                                    m_del[:, cog + g * LC:cog + (g + 1) * LC], AF.Exp,
                                    scale=p_A[:, g * N + n + 2:g * N + n + 3]))
                            D_exp[(t, n + 2)] = sA.c

                    @block.vector
                    def _(v, t=t, n=n, npar=npar, hoff=hoff):
                        if n == N - 1:
                            _w(v, (sV, sV.c))
                        hlast = bass.AP(m_h, hoff + LC - 1, [[2 * GL, 128], [LC, G], [1, 1]])
                        stv = bass.AP(p_st, n, [[G * N, 128], [N, G], [1, 1]])
                        sV.tick(v.tensor_copy(stv, hlast))
                        _w(v, bcC[(t, n)])
                        h3 = m_h[:, hoff:hoff + GL].rearrange("p (g l) -> p g l", g=G)
                        t3 = m_t[:, hoff:hoff + GL].rearrange("p (g l) -> p g l", g=G)
                        C3 = m_Cs[:, npar * LC:(npar + 1) * LC].unsqueeze(1).broadcast_to(
                            [128, G, LC])
                        sV.tick(v.tensor_tensor(t3, h3, C3, ALU.mult))
                        D_yC[(t, n)] = sV.c
                        _w(v, (sV, sV.c))
                        if n == 0:
                            _w(v, (sP, D_ytrend.get(t - 1, 0)))
                            sV.tick(v.tensor_copy(m_y[:, :], m_t[:, hoff:hoff + GL]))
                        else:
                            sV.tick(v.tensor_add(m_y[:, :], m_y[:, :],
                                                 m_t[:, hoff:hoff + GL]))
                        D_add[(t, n)] = sV.c

                    if n + 2 < N:
                        @block.scalar
                        def _(sync, t=t, n=n, npar=npar):
                            if n % 2 == 0:
                                _w(sync, (sV, D_dbu[(t, n + 1)]))
                                sbx = semB[bcnt[0] % 2]
                                bcnt[0] += 1
                                sbx.tick(sync.dma_start(
                                    out=m_Bs[:, :].rearrange("p (i l) -> p i l", i=2),
                                    in_=xdf.ap()[t, 130 + n:132 + n, :].unsqueeze(0)
                                        .broadcast_to([128, 2, LC])), 16)
                                bcB[(t, n + 2)] = bcB[(t, n + 3)] = (sbx, sbx.c)
                            _w(sync, (sV, D_yC[(t, n)]))
                            scx = semC[ccnt[0] % 2]
                            ccnt[0] += 1
                            scx.tick(sync.dma_start(
                                out=m_Cs[:, npar * LC:(npar + 1) * LC],
                                in_=xdf.ap()[t, 128 + N + n + 2:129 + N + n + 2, :]
                                    .broadcast_to([128, LC])), 16)
                            bcC[(t, n + 2)] = (scx, scx.c)

                @block.vector
                def _(v, t=t, uoff=uoff):
                    _w(v, (sV, D_add[(t, N - 1)]))
                    for g in range(G):
                        sV.tick(v.scalar_tensor_tensor(
                            m_y[:, g * LC:(g + 1) * LC],
                            m_u[:, uoff + g * LC:uoff + (g + 1) * LC],
                            p_D[:, g:g + 1],
                            m_y[:, g * LC:(g + 1) * LC], ALU.mult, ALU.add))
                D_yfin[t] = sV.c
                ufree[t] = sV.c

            def emit_B2_tail(t):
                kpass = t // CHG
                for j in range(TPC):
                    jp = j % 2

                    @block.tensor
                    def _(tn, t=t, j=j):
                        if j == 0:
                            _w(tn, (sV, D_yfin[t]), (sA, D_spact[t]))
                            _w(tn, (sA, xbuf_evac_last[0]))
                        else:
                            _w(tn, (sA, yev[(t, j - 1)]))
                        for g in range(G):
                            ins = tn.transpose(
                                pTg(g), m_y[:, g * LC + 128 * j:g * LC + 128 * (j + 1)],
                                p_id[:, :])
                        sP.tick(ins)
                        ytr[(t, j)] = sP.c

                    @block.scalar
                    def _(sc, t=t, j=j, jp=jp):
                        _w(sc, (sP, ytr[(t, j)]))
                        if j < 2:
                            _w(sc, (sX, D_ydma.get((t - 1, j + 2), 0)))
                        else:
                            _w(sc, (sX, D_ydma[(t, j - 2)]))
                        for g in range(G):
                            sA.tick(sc.copy(m_yr[:, jp * DC + 128 * g:jp * DC + 128 * (g + 1)],
                                            pTg(g)))
                        yev[(t, j)] = sA.c

                    @block.sync
                    def _(sync, t=t, j=j, jp=jp, kpass=kpass):
                        _w(sync, (sA, yev[(t, j)]))
                        Jt = (t % CHG) * TPC + j
                        shard, slot = Jt % NCORES, Jt // NCORES
                        r0 = shard * (MT * 128) + slot * 128
                        sX.tick(sync.dma_start(
                            out=ya2a_i.ap()[kpass, r0:r0 + 128, :],
                            in_=m_yr[:, jp * DC:(jp + 1) * DC]), 16)
                        _w(sync, (sX, sX.c))
                        D_ydma[(t, j)] = sX.c
                D_ytrend[t] = ytr[(t, TPC - 1)]
                D_ydend[t] = D_ydma[(t, TPC - 1)]

                if (t + 1) % CHG == 0:
                    @block.gpsimd
                    def _(g_, t=t, kpass=kpass):
                        _w(g_, (sX, D_ydend[t]))
                        with allow_ici_all_to_all():
                            sC.tick(g_.collective_compute(
                                "AllToAll", ALU.bypass, replica_groups=rg,
                                ins=[ya2a_i.ap()[kpass].opt()],
                                outs=[ya2a_o.ap()[kpass].opt()]), 1)
                        a2a_done[kpass] = sC.c

            HS = H // 512
            DS = D // 512
            pW = [pB0, pB1, pC0, pC1]
            pFT = pF0
            fst = {}

            def emit_ln_rowmajor(srcap):
                @block.vector
                def _(v):
                    _w(v, (sV, sV.c))
                    for i in range(DB):
                        sV.tick(v.bn_stats(f_bn[:, 6 * i:6 * (i + 1)],
                                           srcap[:, 512 * i:512 * (i + 1)]))
                    _w(v, (sV, sV.c))
                    sV.tick(v.bn_aggr(f_sg[:, 0:2], f_bn[:, 0:6 * DB]))

                @block.scalar
                def _(sc):
                    _w(sc, (sV, sV.c))
                    sA.tick(sc.activation(f_sg[:, 4:5], f_sg[:, 1:2], AF.Ln,
                                          bias=p_eps[:, 0:1]))
                    _w(sc, (sA, sA.c))
                    sA.tick(sc.activation(f_sg[:, 3:4], f_sg[:, 4:5], AF.Exp, scale=-0.5))

                @block.vector
                def _(v):
                    _w(v, (sA, sA.c), (sV, sV.c))
                    sV.tick(v.tensor_scalar_mul(f_sg[:, 2:3], f_sg[:, 0:1], -1.0))
                    _w(v, (sV, sV.c))
                    sV.tick(v.tensor_mul(f_sg[:, 5:6], f_sg[:, 2:3], f_sg[:, 3:4]))

            def emit_prep(kp, m):
                s = fst.setdefault(kp, {})

                @block.sync
                def _(sync, kp=kp, m=m, s=s):
                    _w(sync, (sC, a2a_done[kp]))
                    _w(sync, (sP, fst.get(kp - 1, {}).get('h2T_end', 0)))
                    _w(sync, (sV, fst.get(kp - 1, {}).get('o3v1', 0)))
                    if m == 1:
                        _w(sync, (sV, s['pool_free0']))
                    for i in range(NCORES):
                        r0 = i * RW + m * 128
                        sX.tick(sync.dma_start(out=f_yt[:, i * DC:(i + 1) * DC],
                                               in_=ya2a_o.ap()[kp, r0:r0 + 128, :]), 16)
                    sX.tick(sync.dma_start(out=f_xr[:, :], in_=xrows[kp * MT + m]), 16)
                    _w(sync, (sX, sX.c))
                yt_dma = sX.c

                @block.vector
                def _(v, yt_dma=yt_dma):
                    _w(v, (sX, yt_dma))
                emit_ln_rowmajor(f_yt)

                @block.vector
                def _(v, kp=kp, m=m):
                    _w(v, (sV, sV.c))
                    if m == 0:
                        _w(v, (sX, fst.get(kp - 1, {}).get('out_dma', 0)))
                    sV.tick(v.tensor_scalar(f_t[:, :], f_yt[:, :], f_sg[:, 3:4],
                                            f_sg[:, 5:6], ALU.mult, ALU.add))
                    _w(v, (sV, sV.c))
                    sV.tick(v.tensor_add(f_yt[:, :], f_xr[:, :], f_t[:, :]))
                xm_tick = sV.c

                @block.sync
                def _(sync, kp=kp, m=m, xm_tick=xm_tick, s=s):
                    _w(sync, (sV, xm_tick))
                    _w(sync, (sX, fst.get(kp - 2, {}).get('xm_ld', 0)))
                    sX.tick(sync.dma_start(out=xmid_sp.ap()[kp % 2, m], in_=f_yt[:, :]), 16)
                    _w(sync, (sX, sX.c))
                    s['spill%d' % m] = sX.c

                emit_ln_rowmajor(f_yt)

                @block.vector
                def _(v, m=m):
                    _w(v, (sV, sV.c))
                    f_x2 = f_t.ap().bitcast(BF16)[:, 0:D]
                    sV.tick(v.tensor_scalar(f_x2, f_yt[:, :], f_sg[:, 3:4],
                                            f_sg[:, 5:6], ALU.mult, ALU.add))
                x2_tick = sV.c
                s['pool_free%d' % m] = x2_tick

                for kt2 in range(KT1 // 4):
                    @block.tensor
                    def _(tn, m=m, kt2=kt2, x2_tick=x2_tick):
                        _w(tn, (sV, x2_tick), (sA, sA.c))
                        for q in range(4):
                            kt = kt2 * 4 + q
                            ins = tn.transpose(
                                pFT.ap().bitcast(BF16)[:, 256 * q:256 * q + 128],
                                f_t.ap().bitcast(BF16)[:, 128 * kt:128 * (kt + 1)],
                                p_idb[:, :])
                        sP.tick(ins)
                    tr4 = sP.c

                    @block.scalar
                    def _(sc, kp=kp, m=m, kt2=kt2, tr4=tr4):
                        _w(sc, (sP, tr4))
                        if m == 0 and kt2 == 0:
                            _w(sc, (sX, fst.get(kp - 1, {}).get('out_dma', 0)))
                        dst = bass.AP(f_x2T, kt2 * 4 * RW + 128 * m,
                                      [[KT1 * RW, 128], [RW, 4], [1, 128]])
                        src = pFT.ap().bitcast(BF16)[:, 0:1024].rearrange(
                            "p (q j) -> p q j", q=4)[:, :, 0:128]
                        sA.tick(sc.copy(dst, src))
                s['x2Ttr%d' % m] = sP.c
                s['x2T%d' % m] = sA.c

            def emit_W1(kp, half):
                s = fst[kp]
                evt = s.setdefault('w1evt', {})
                sl = s.setdefault('w1slab', {})
                for mg in range(half * (NS1 // 2), (half + 1) * (NS1 // 2)):
                    wpar = mg % 2

                    @block.sync
                    def _(sync, kp=kp, mg=mg, wpar=wpar, s=s):
                        _w(sync, (sP, s.get('w1used%d' % (mg - 2), 0)))
                        if mg < 2:
                            _w(sync, (sP, fst.get(kp - 1, {}).get('w1end', 0)))
                        if kp == 0:
                            _w(sync, (sG, CW1[mg // 4]))
                        swx = semW[wcnt[0] % 4]
                        wcnt[0] += 1
                        swx.tick(sync.dma_start(
                            out=f_ws1[:, wpar * 4096:(wpar + 1) * 4096],
                            in_=wt1b.ap()[mg]), 16)
                        sl[mg] = (swx, swx.c)

                    for mb in range(2):
                        mi = mg * 2 + mb
                        pF = pW[mi % 2]

                        @block.tensor
                        def _(tn, kp=kp, mg=mg, mb=mb, mi=mi, pF=pF, wpar=wpar, s=s):
                            _w(tn, sl[mg])
                            _w(tn, (sA, max(s['x2T0'], s['x2T1'], evt.get(mi - 2, 0))))
                            if mi == 0:
                                _w(tn, (sV, fst.get(kp - 1, {}).get('w3stats', 0)))
                                _w(tn, (sA, fst.get(kp - 1, {}).get('w3evac', 0)))
                            for kt in range(KT1):
                                sP.tick(tn.matmul(
                                    pF[:, 0:RW],
                                    f_ws1[:, wpar * 4096 + (kt * 2 + mb) * 128:
                                          wpar * 4096 + (kt * 2 + mb + 1) * 128],
                                    f_x2T[:, kt * RW:(kt + 1) * RW],
                                    start=(kt == 0), stop=(kt == KT1 - 1),
                                    skip_group_check=True))
                        mm_done = sP.c

                        @block.scalar
                        def _(sc, kp=kp, mi=mi, pF=pF, mm_done=mm_done):
                            _w(sc, (sP, mm_done))
                            if mi == 0:
                                _w(sc, (sP, fst.get(kp - 1, {}).get('w3mm_end', 0)))
                            sA.tick(sc.activation(f_h1[:, mi * RW:(mi + 1) * RW],
                                                  pF[:, 0:RW], AF.Relu,
                                                  bias=p_b1[:, mi:mi + 1]))
                            evt[mi] = sA.c
                    s['w1used%d' % mg] = sP.c
                if half == 1:
                    s['w1end'] = sP.c
                    s['w1evt_end'] = sA.c

            def emit_W2(kp, hp):
                s = fst[kp]
                BT = 2
                slab, mmb = {}, {}
                for bt in range(KTH // BT):
                    @block.sync
                    def _(sync, kp=kp, hp=hp, bt=bt, s=s):
                        _w(sync, (sP, mmb.get(bt - 2, 0)))
                        if bt < 2:
                            _w(sync, (sP, s.get('w2mm%d' % (hp - 1), 0)))
                            if hp == 0:
                                _w(sync, (sP, fst.get(kp - 1, {}).get('w3mm_end', 0)))
                        if kp == 0:
                            _w(sync, (sG, CW2[(hp * 64 + bt * 4) // 32]))
                        s0 = hp * 64 + bt * 4
                        swx = semW[wcnt[0] % 4]
                        wcnt[0] += 1
                        swx.tick(sync.dma_start(
                            out=f_ws23[:, (bt % 2) * 2048:(bt % 2) * 2048 + 2048]
                                .rearrange("p (i j) -> p i j", i=4),
                            in_=wt2b.ap()[s0:s0 + 4].rearrange("i p j -> p i j")), 16)
                        slab[bt] = (swx, swx.c)

                    @block.tensor
                    def _(tn, kp=kp, hp=hp, bt=bt, s=s):
                        _w(tn, slab[bt])
                        if bt == 0:
                            _w(tn, (sA, s['w1evt_end']))
                            _w(tn, (sA, s.get('w2evac%d' % (hp - 1), 0)))
                        for kt in range(bt * BT, (bt + 1) * BT):
                            for m in range(2):
                                for sx2 in range(2):
                                    slot = (kt % (2 * BT)) * 2 + sx2
                                    sP.tick(tn.matmul(
                                        pW[2 * m + sx2][:, 0:512],
                                        f_h1[:, kt * RW + m * 128:kt * RW + (m + 1) * 128],
                                        f_ws23[:, slot * 512:(slot + 1) * 512],
                                        start=(kt == 0), stop=False,
                                        skip_group_check=True))
                        mmb[bt] = sP.c

                @block.sync
                def _(sync, kp=kp, hp=hp, s=s):
                    _w(sync, (sP, s.get('bmm%d' % (hp - 1), 0)))
                    if kp == 0 and hp == 0:
                        _w(sync, (sG, CAST_B))
                    swx = semW[wcnt[0] % 4]
                    wcnt[0] += 1
                    swx.tick(sync.dma_start(
                        out=p_bs[0:1, :], in_=b2bd.ap()[0:1, hp * 1024:(hp + 1) * 1024]), 16)
                    s['bld%d' % hp] = (swx, swx.c)

                @block.tensor
                def _(tn, kp=kp, hp=hp, s=s):
                    _w(tn, s['bld%d' % hp])
                    for m in range(2):
                        for sx2 in range(2):
                            sP.tick(tn.matmul(pW[2 * m + sx2][:, 0:512], p_onesb[:, :],
                                              p_bs[0:1, sx2 * 512:(sx2 + 1) * 512],
                                              start=False, stop=True,
                                              skip_group_check=True))
                s['w2mm%d' % hp] = sP.c
                s['bmm%d' % hp] = sP.c

                @block.scalar
                def _(sc, kp=kp, hp=hp, s=s):
                    _w(sc, (sP, s['w2mm%d' % hp]))
                    if hp == 0:
                        _w(sc, (sV, s['pool_free1']))
                        _w(sc, (sX, s['spill1']))
                    for m in range(2):
                        for sx2 in range(2):
                            hs = hp * 2 + sx2
                            sA.tick(sc.activation(
                                f_h2r[:, m * 4096 + hs * 512:m * 4096 + (hs + 1) * 512],
                                pW[2 * m + sx2][:, 0:512], AF.Relu))
                s['w2evac%d' % hp] = sA.c

            def emit_h2T(kp):
                s = fst[kp]
                for m in range(2):
                    for q4 in range(8):
                        @block.tensor
                        def _(tn, kp=kp, m=m, q4=q4, s=s):
                            _w(tn, (sA, sA.c))
                            if m == 0 and q4 == 0:
                                _w(tn, (sA, s['w2evac3']))
                            for q in range(4):
                                ht = q4 * 4 + q
                                ins = tn.transpose(
                                    pFT.ap().bitcast(BF16)[:, 256 * q:256 * q + 128],
                                    f_h2r[:, m * 4096 + ht * 128:m * 4096 + (ht + 1) * 128],
                                    p_idb[:, :])
                            sP.tick(ins)
                        tr = sP.c

                        @block.scalar
                        def _(sc, kp=kp, m=m, q4=q4, tr=tr, s=s):
                            _w(sc, (sP, tr))
                            if m == 0 and q4 == 0:
                                _w(sc, (sP, s['w2mm3']))
                            dst = bass.AP(f_h1, q4 * 4 * RW + m * 128,
                                          [[MB1 * RW, 128], [RW, 4], [1, 128]])
                            src = pFT.ap().bitcast(BF16)[:, 0:1024].rearrange(
                                "p (q j) -> p q j", q=4)[:, :, 0:128]
                            sA.tick(sc.copy(dst, src))
                s['h2T_end'] = sP.c
                s['h2Tev_end'] = sA.c

            def emit_W3(kp):
                s = fst[kp]
                BT3 = 2
                slab, mmb = {}, {}
                for dsp in range(2):
                    for bt in range(KTH // BT3):
                        fb = dsp * (KTH // BT3) + bt

                        @block.sync
                        def _(sync, kp=kp, dsp=dsp, bt=bt, fb=fb, s=s):
                            _w(sync, (sP, mmb.get(fb - 2, 0)))
                            if fb < 2:
                                _w(sync, (sP, s['w2mm3']))
                            if kp == 0:
                                _w(sync, (sG, CW3[(dsp * 64 + bt * 4) // 32]))
                            s0 = dsp * 64 + bt * 4
                            swx = semW[wcnt[0] % 4]
                            wcnt[0] += 1
                            swx.tick(sync.dma_start(
                                out=f_ws23[:, (fb % 2) * 2048:(fb % 2) * 2048 + 2048]
                                    .rearrange("p (i j) -> p i j", i=4),
                                in_=wt3b.ap()[s0:s0 + 4].rearrange("i p j -> p i j")), 16)
                            slab[(dsp, bt)] = (swx, swx.c)

                        @block.tensor
                        def _(tn, kp=kp, dsp=dsp, bt=bt, fb=fb, s=s):
                            _w(tn, slab[(dsp, bt)])
                            if bt == 0:
                                _w(tn, (sA, s['h2Tev_end']))
                                if dsp == 0:
                                    _w(tn, (sA, s['w2evac3']))
                                else:
                                    _w(tn, (sA, s['w3evac0']))
                            for kt in range(bt * BT3, (bt + 1) * BT3):
                                for m in range(2):
                                    for sx2 in range(2):
                                        slot = (kt % (2 * BT3)) * 2 + sx2
                                        sP.tick(tn.matmul(
                                            pW[2 * m + sx2][:, 0:512],
                                            f_h2T[:, kt * RW + m * 128:kt * RW + (m + 1) * 128],
                                            f_ws23[:, slot * 512:(slot + 1) * 512],
                                            start=(kt == 0), stop=False,
                                            skip_group_check=True))
                            mmb[fb] = sP.c

                    @block.sync
                    def _(sync, kp=kp, dsp=dsp, s=s):
                        _w(sync, (sP, s['bmm3'] if dsp == 0 else s['w3mm0']))
                        swx = semW[wcnt[0] % 4]
                        wcnt[0] += 1
                        swx.tick(sync.dma_start(
                            out=p_bs[0:1, :],
                            in_=b3bd.ap()[0:1, dsp * 1024:(dsp + 1) * 1024]), 16)
                        s['b3ld%d' % dsp] = (swx, swx.c)

                    @block.tensor
                    def _(tn, kp=kp, dsp=dsp, s=s):
                        _w(tn, s['b3ld%d' % dsp])
                        for m in range(2):
                            for sx2 in range(2):
                                sP.tick(tn.matmul(pW[2 * m + sx2][:, 0:512], p_onesb[:, :],
                                                  p_bs[0:1, sx2 * 512:(sx2 + 1) * 512],
                                                  start=False, stop=True,
                                                  skip_group_check=True))
                    s['w3mm%d' % dsp] = sP.c

                    @block.scalar
                    def _(sc, kp=kp, dsp=dsp, s=s):
                        _w(sc, (sP, s['w3mm%d' % dsp]))
                        if dsp == 0:
                            _w(sc, (sP, s['x2Ttr1']))
                            _w(sc, (sP, s['w1end']))
                        for m in range(2):
                            stage = f_t if m == 0 else f_o3b
                            for sx2 in range(2):
                                ds = dsp * 2 + sx2
                                sA.tick(sc.copy(stage[:, ds * 512:(ds + 1) * 512],
                                                pW[2 * m + sx2][:, 0:512]))
                    s['w3evac%d' % dsp] = sA.c
                s['w3mm_end'] = s['w3mm1']
                s['w3evac'] = s['w3evac1']

            def emit_W3_fin(kp):
                s = fst[kp]

                @block.vector
                def _(v, kp=kp, s=s):
                    _w(v, (sA, s['w3evac1']))
                    for m in range(2):
                        stage = f_t if m == 0 else f_o3b
                        for ds in range(4):
                            sl8 = m * 4 + ds
                            sV.tick(v.bn_stats(f_bn[:, 6 * sl8:6 * (sl8 + 1)],
                                               stage[:, ds * 512:(ds + 1) * 512]))
                    _w(v, (sV, sV.c))
                    for m in range(2):
                        sV.tick(v.bn_aggr(f_sg[:, 8 * m:8 * m + 2],
                                          f_bn[:, 24 * m:24 * m + 24]))
                s['w3stats1'] = sV.c
                s['w3stats'] = sV.c
                ag_v = sV.c

                @block.scalar
                def _(sc, kp=kp, ag_v=ag_v, s=s):
                    _w(sc, (sA, s['w3evac1']), (sV, ag_v))
                    for m in range(2):
                        o = 8 * m
                        sA.tick(sc.activation(f_sg[:, o + 4:o + 5], f_sg[:, o + 1:o + 2],
                                              AF.Ln, bias=p_eps[:, 0:1]))
                        _w(sc, (sA, sA.c))
                        sA.tick(sc.activation(f_sg[:, o + 3:o + 4], f_sg[:, o + 4:o + 5],
                                              AF.Exp, scale=-0.5))
                ln_a = sA.c

                @block.sync
                def _(sync, kp=kp, s=s):
                    _w(sync, (sP, s['h2T_end']))
                    sX.tick(sync.dma_start(out=f_yt[:, :], in_=xmid_sp.ap()[kp % 2, 0]), 16)
                    sX.tick(sync.dma_start(out=f_xr[:, :], in_=xmid_sp.ap()[kp % 2, 1]), 16)
                    _w(sync, (sX, sX.c))
                s['xm_ld'] = sX.c

                @block.vector
                def _(v, kp=kp, ln_a=ln_a, s=s):
                    _w(v, (sA, ln_a))
                    for m in range(2):
                        o = 8 * m
                        sV.tick(v.tensor_scalar_mul(f_sg[:, o + 2:o + 3],
                                                    f_sg[:, o:o + 1], -1.0))
                        _w(v, (sV, sV.c))
                        sV.tick(v.tensor_mul(f_sg[:, o + 5:o + 6], f_sg[:, o + 2:o + 3],
                                             f_sg[:, o + 3:o + 4]))
                    _w(v, (sX, s['xm_ld']))
                    for m in range(2):
                        o = 8 * m
                        stage = f_t if m == 0 else f_o3b
                        xm = f_yt if m == 0 else f_xr
                        _w(v, (sV, sV.c))
                        sV.tick(v.tensor_scalar(stage[:, 0:D], stage[:, 0:D],
                                                f_sg[:, o + 3:o + 4], f_sg[:, o + 5:o + 6],
                                                ALU.mult, ALU.add))
                        _w(v, (sV, sV.c))
                        sV.tick(v.tensor_add(stage[:, 0:D], stage[:, 0:D], xm[:, :]))
                        if m == 0:
                            s['o3v0'] = sV.c
                    s['o3v1'] = sV.c

                @block.sync
                def _(sync, kp=kp, s=s):
                    _w(sync, (sV, s['o3v0']))
                    sX.tick(sync.dma_start(out=out[kp * MT + 0], in_=f_t[:, 0:D]), 16)
                    _w(sync, (sV, s['o3v1']))
                    sX.tick(sync.dma_start(out=out[kp * MT + 1], in_=f_o3b[:, 0:D]), 16)
                    _w(sync, (sX, sX.c))
                s['out_dma'] = sX.c

            pending = []
            emit_B1(0)
            emit_B1(1)
            emit_AR(0)
            for t in range(NCHT):
                if t + 2 < NCHT:
                    emit_B1(t + 2)
                emit_B2(t)
                if pending:
                    pending.pop(0)()
                emit_B2_tail(t)
                if (t + 1) % CHG == 0:
                    kp = t // CHG
                    pending += [
                        lambda kp=kp: ((emit_W3_fin(kp - 1) if kp > 0 else None),
                                       emit_prep(kp, 0), emit_prep(kp, 1),
                                       emit_W1(kp, 0)),
                        lambda kp=kp: (emit_W1(kp, 1), emit_W2(kp, 0)),
                        lambda kp=kp: (emit_W2(kp, 1), emit_W2(kp, 2)),
                        lambda kp=kp: (emit_W2(kp, 3), emit_h2T(kp), emit_W3(kp)),
                    ]
            while pending:
                pending.pop(0)()
            emit_W3_fin(NPASS - 1)

            @block.sync
            def _(sync):
                _w(sync, (sX, sX.c))

    return nc



def _host_inputs(cfg: Cfg, x, conv_w, conv_b, xproj_w, dtproj_w, dtproj_b,
                 A_log, D_skip, W1, b1, W2, b2, W3, b3):
    B, LSEQ, D, H = cfg.B, cfg.LSEQ, cfg.D, cfg.H
    DC, G, N, K = cfg.DC, cfg.G, cfg.N, cfg.K
    NPASS, MT = cfg.NPASS, cfg.MT
    f32 = np.float32
    C = np.ascontiguousarray

    ident = np.eye(128, dtype=f32)
    xf = x.reshape(B * LSEQ, D)
    w1t = W1.T
    w2t = W2.T
    w3t = W3.T
    KT1, KTH, MB1 = cfg.KT1, cfg.KTH, cfg.MB1
    w1s = C(w1t.reshape(KT1, 128, MB1 // 2, 256).transpose(2, 1, 0, 3)
            .reshape(MB1 // 2, 128, KT1 * 256).astype(f32))
    w2s = C(w2t.reshape(KTH, 128, H // 1024, 2, 512).transpose(2, 0, 3, 1, 4)
            .reshape(KTH * (H // 512), 128, 512).astype(f32))
    w3s = C(w3t.reshape(KTH, 128, D // 1024, 2, 512).transpose(2, 0, 3, 1, 4)
            .reshape(KTH * (D // 512), 128, 512).astype(f32))

    def perm_g(a):
        t = a.reshape(G, 128, -1).transpose(1, 0, 2).reshape(128, -1)
        return C(t.astype(f32))

    in_maps = []
    for c in range(NCORES):
        sl = slice(DC * c, DC * (c + 1))
        PRc = cfg.PR
        tiles = []
        for kp in range(NPASS):
            for m in range(MT):
                r0 = PRc * kp + 128 * (c + NCORES * m)
                tiles.append(xf[r0:r0 + 128])
        xrows = np.stack(tiles, axis=0)

        d = {
            "xsl": C(x[:, :, sl]),
            "xrows": C(xrows.astype(f32)),
            "wxpT": C(xproj_w[:, sl].T.astype(f32)),
            "wdtT": C(dtproj_w[sl, :].T.astype(f32)),
            "cw": perm_g(conv_w[sl, 0, :]),
            "cb": perm_g(conv_b[sl]),
            "dtb": perm_g(dtproj_b[sl]),
            "alog": perm_g(A_log[sl]),
            "dsk": perm_g(D_skip[sl]),
            "w1f": w1s,
            "w2f": w2s,
            "w3f": w3s,
            "b1r": C(b1.reshape(cfg.MB1, 128).T.astype(f32)),
            "b2row": C(b2.reshape(1, cfg.H).astype(f32)),
            "b3r": C(b3.reshape(1, D).astype(f32)),
            "ident": ident,
        }
        in_maps.append(d)
    return in_maps


def _assemble(cfg: Cfg, results):
    B, LSEQ, D = cfg.B, cfg.LSEQ, cfg.D
    out = np.empty((B * LSEQ, D), dtype=np.float32)
    for c in range(NCORES):
        o = np.asarray(results[c]["out"])
        i = 0
        for kp in range(cfg.NPASS):
            for m in range(cfg.MT):
                r0 = cfg.PR * kp + 128 * (c + NCORES * m)
                out[r0:r0 + 128] = o[i]
                i += 1
    return out.reshape(B, LSEQ, D)


_CFG = Cfg()


def kernel(L0_attn_x, L0_attn_conv_w, L0_attn_conv_b, L0_attn_xproj_w,
           L0_attn_dtproj_w, L0_attn_dtproj_b, L0_attn_A_log, L0_attn_D_skip,
           L0_ffn_W1, L0_ffn_b1, L0_ffn_W2, L0_ffn_b2, L0_ffn_W3, L0_ffn_b3):
    cfg = _CFG
    args = [np.asarray(a, dtype=np.float32) for a in
            (L0_attn_x, L0_attn_conv_w, L0_attn_conv_b, L0_attn_xproj_w,
             L0_attn_dtproj_w, L0_attn_dtproj_b, L0_attn_A_log, L0_attn_D_skip,
             L0_ffn_W1, L0_ffn_b1, L0_ffn_W2, L0_ffn_b2, L0_ffn_W3, L0_ffn_b3)]
    in_maps = _host_inputs(cfg, *args)
    nc = build(cfg)
    res = run_bass_kernel_spmd(nc, in_maps, core_ids=list(range(NCORES)))
    return _assemble(cfg, res.results)


# version 31
# speedup vs baseline: 1.1696x; 1.0421x over previous
import numpy as np
from contextlib import ExitStack

import concourse.bass as bass
import concourse.mybir as mybir
from concourse.bass_utils import run_bass_kernel_spmd
from concourse.aot_env import allow_ici_all_to_all

F32 = mybir.dt.float32
F32R = mybir.dt.float32r
BF16 = mybir.dt.bfloat16
AF = mybir.ActivationFunctionType
ALU = mybir.AluOpType
AX = mybir.AxisListType

NCORES = 8


class Cfg:
    def __init__(self, B=2, LSEQ=8192, D=2048, H=4096, LC=512, NPASS=8):
        self.B, self.LSEQ, self.D, self.H, self.LC, self.NPASS = B, LSEQ, D, H, LC, NPASS
        self.R, self.N, self.K = 128, 16, 4
        self.DC = D // NCORES
        self.G = self.DC // 128
        self.NCH = LSEQ // LC
        self.NCHT = self.B * self.NCH
        self.NTL = (self.B * LSEQ) // 128
        self.TPC = LC // 128
        self.ROWS = B * LSEQ
        self.PR = self.ROWS // NPASS
        self.MT = self.PR // NCORES // 128
        self.CHG = self.NCHT // NPASS
        self.KT1 = D // 128
        self.KTH = H // 128
        self.MB1 = H // 128
        self.MBH = H // 128
        assert self.G >= 1 and self.TPC >= 1 and self.MT >= 1 and LC % 512 == 0


class Sem:
    def __init__(self, es, nc, name):
        self.h = es.enter_context(nc.semaphore(name))
        self.c = 0

    def tick(self, ins, n=1):
        ins.then_inc(self.h, n)
        self.c += n
        return self.c


def _w(eng, *deps):
    for s, v in deps:
        assert v <= s.c, f"wait on future count {v} > emitted {s.c}"
        if v > 0:
            eng.wait_ge(s.h, v)


def _wf(eng, *deps):
    for s, v in deps:
        if v > 0:
            eng.wait_ge(s.h, v)


def build(cfg: Cfg):
    nc = bass.Bass("TRN2", target_bir_lowering=False, debug=False)
    B, LSEQ, D, H, LC = cfg.B, cfg.LSEQ, cfg.D, cfg.H, cfg.LC
    G, N, R, DC, K = cfg.G, cfg.N, cfg.R, cfg.DC, cfg.K
    NCH, NCHT, TPC, NTL = cfg.NCH, cfg.NCHT, cfg.TPC, cfg.NTL
    NPASS, MT, CHG = cfg.NPASS, cfg.MT, cfg.CHG
    KT1, KTH, MB1 = cfg.KT1, cfg.KTH, cfg.MB1
    MBH = cfg.MBH = H // 128
    GL = G * LC
    FT = R + 2 * N
    NLB = LC // 512
    DB = D // 512

    def inp(name, shape, dt=F32):
        return nc.dram_tensor(name, list(shape), dt, kind="ExternalInput").ap()

    xsl = inp("xsl", [B, LSEQ, DC])
    xrows = inp("xrows", [NPASS * MT, 128, D])
    wxpT = inp("wxpT", [DC, FT])
    wdtT = inp("wdtT", [R, DC])
    cw = inp("cw", [128, G * K])
    cb = inp("cb", [128, G])
    dtb = inp("dtb", [128, G])
    alog = inp("alog", [128, G * N])
    dsk = inp("dsk", [128, G])
    NS1 = MB1 // 2
    NS2 = KTH * (H // 512)
    NS3 = KTH * (D // 512)
    w1f = inp("w1f", [NS1, 128, 4096])
    w2f = inp("w2f", [NS2, 128, 512])
    w3f = inp("w3f", [NS3, 128, 512])
    b1r = inp("b1r", [128, MB1])
    b2row = inp("b2row", [1, H])
    b3r = inp("b3r", [1, D])
    ident = inp("ident", [128, 128])

    out = nc.dram_tensor("out", [NPASS * MT, 128, D], F32, kind="ExternalOutput").ap()

    b2bd = nc.dram_tensor("b2bd", [1, H], BF16)
    b3bd = nc.dram_tensor("b3bd", [1, D], BF16)
    wt1b = nc.dram_tensor("wt1b", [NS1, 128, 4096], BF16)
    wt2b = nc.dram_tensor("wt2b", [NS2, 128, 512], BF16)
    wt3b = nc.dram_tensor("wt3b", [NS3, 128, 512], BF16)
    statp = nc.dram_tensor("statp", [128, 2 * NTL], F32)
    statf = nc.dram_tensor("statf", [128, 2 * NTL], F32)
    xdp = nc.dram_tensor("xdp", [NCHT, FT, LC], F32)
    xdf = nc.dram_tensor("xdf", [NCHT, FT, LC], F32)
    ya2a_i = nc.dram_tensor("ya2a_i", [NPASS, NCORES * MT * 128, DC], F32)
    ya2a_o = nc.dram_tensor("ya2a_o", [NPASS, NCORES * MT * 128, DC], F32)
    xmid_sp = nc.dram_tensor("xmid_sp", [2, MT, 128, D], F32)

    rg = [list(range(NCORES))]

    es = ExitStack()
    E = es.enter_context
    sX = Sem(es, nc, "sX")
    sG = Sem(es, nc, "sG")
    sC = Sem(es, nc, "sC")
    sP = Sem(es, nc, "sP")
    sV = Sem(es, nc, "sV")
    sA = Sem(es, nc, "sA")
    sO = Sem(es, nc, "sO")
    sB = Sem(es, nc, "sB")
    sB1 = Sem(es, nc, "sB1")
    sC0 = Sem(es, nc, "sC0")
    sC1 = Sem(es, nc, "sC1")
    sW = Sem(es, nc, "sW")
    sW1 = Sem(es, nc, "sW1")
    sW2 = Sem(es, nc, "sW2")
    sW3 = Sem(es, nc, "sW3")
    sGI0 = Sem(es, nc, "sGI0")
    sGI1 = Sem(es, nc, "sGI1")
    sGO0 = Sem(es, nc, "sGO0")
    sGO1 = Sem(es, nc, "sGO1")
    semGI = [sGI0, sGI1]
    semGO = [sGO0, sGO1]
    semB = [sB, sB1]
    semC = [sC0, sC1]
    semW = [sW, sW1, sW2, sW3]
    bcnt = [0]
    ccnt = [0]
    wcnt = [0]
    CW1, CW2, CW3 = {}, {}, {}

    p_id = E(nc.sbuf_tensor("p_id", [128, 128], F32))
    p_idb = E(nc.sbuf_tensor("p_idb", [128, 128], BF16))
    p_wxpTf = E(nc.sbuf_tensor("p_wxpTf", [128, G * FT], F32))
    p_wxpT = E(nc.sbuf_tensor("p_wxpT", [128, G * FT], F32R))
    p_wdtTf = E(nc.sbuf_tensor("p_wdtTf", [128, DC], F32))
    p_wdtT = E(nc.sbuf_tensor("p_wdtT", [128, DC], F32R))
    p_cw = E(nc.sbuf_tensor("p_cw", [128, G * K], F32))
    p_cb = E(nc.sbuf_tensor("p_cb", [128, G], F32))
    p_dtb = E(nc.sbuf_tensor("p_dtb", [128, G], F32))
    p_A = E(nc.sbuf_tensor("p_A", [128, G * N], F32))
    p_D = E(nc.sbuf_tensor("p_D", [128, G], F32))
    p_b1 = E(nc.sbuf_tensor("p_b1", [128, MB1], F32))
    p_ones = E(nc.sbuf_tensor("p_ones", [1, 128], F32))
    p_onesb = E(nc.sbuf_tensor("p_onesb", [1, 128], BF16))
    p_bs = E(nc.sbuf_tensor("p_bs", [1, 2 * 512], BF16))
    p_st = E(nc.sbuf_tensor("p_st", [128, G * N], F32))
    p_sum = E(nc.sbuf_tensor("p_sum", [128, NTL], F32))
    p_sq = E(nc.sbuf_tensor("p_sq", [128, NTL], F32))
    p_nm = E(nc.sbuf_tensor("p_nm", [128, NTL], F32))
    p_rs = E(nc.sbuf_tensor("p_rs", [128, NTL], F32))
    p_tmp = E(nc.sbuf_tensor("p_tmp", [128, NTL], F32))
    p_eps = E(nc.sbuf_tensor("p_eps", [128, 1], F32))

    pB0 = E(nc.psum_tensor("pB0", [128, 512], F32))
    pB1 = E(nc.psum_tensor("pB1", [128, 512], F32))
    pC0 = E(nc.psum_tensor("pC0", [128, 512], F32))
    pC1 = E(nc.psum_tensor("pC1", [128, 512], F32))
    pF0 = E(nc.psum_tensor("pF0", [128, 512], F32))
    pF1 = E(nc.psum_tensor("pF1", [128, 512], F32))
    pP = E(nc.psum_tensor("pP", [128, 512], F32))
    pT = E(nc.psum_tensor("pT", [128, 512], F32))

    def pTg(g):
        return pT[:, 0:128] if G == 1 else pT[:, 128 * g:128 * (g + 1)]

    with es, nc.Block() as block:
        @block.sync
        def _(sync):
            for dst, src in [
                (p_id[:, :], ident),
                (p_wxpTf[:, :].rearrange("p (g f) -> p g f", g=G),
                 wxpT.rearrange("(g p) f -> p g f", p=128)),
                (p_wdtTf[:, :], wdtT),
                (p_cw[:, :], cw), (p_cb[:, :], cb), (p_dtb[:, :], dtb),
                (p_A[:, :], alog), (p_D[:, :], dsk),
                (p_b1[:, :], b1r),
            ]:
                sX.tick(sync.dma_start(out=dst, in_=src), 16)

        @block.gpsimd
        def _(g):
            sG.tick(g.dma_start(out=b2bd.ap(), in_=b2row), 16)
            sG.tick(g.dma_start(out=b3bd.ap(), in_=b3r), 16)
            for pc in range(4):
                sG.tick(g.dma_start(out=wt1b.ap()[4 * pc:4 * pc + 4],
                                    in_=w1f[4 * pc:4 * pc + 4]), 16)
                CW1[pc] = sG.c
            for pc in range(8):
                sG.tick(g.dma_start(out=wt2b.ap()[32 * pc:32 * pc + 32],
                                    in_=w2f[32 * pc:32 * pc + 32]), 16)
                CW2[pc] = sG.c
            for pc in range(4):
                sG.tick(g.dma_start(out=wt3b.ap()[32 * pc:32 * pc + 32],
                                    in_=w3f[32 * pc:32 * pc + 32]), 16)
                CW3[pc] = sG.c
        CAST_B = 32

        @block.vector
        def _(v):
            _w(v, (sX, sX.c))
            sV.tick(v.memset(p_ones[:, :], 1.0))
            sV.tick(v.memset(p_onesb[:, :], 1.0))
            sV.tick(v.memset(p_eps[:, :], 1e-5))
            sV.tick(v.memset(p_st[:, :], 0.0))
            sV.tick(v.tensor_copy(p_idb[:, :], p_id[:, :]))
            sV.tick(v.tensor_copy(p_wxpT[:, :], p_wxpTf[:, :]))
            sV.tick(v.tensor_copy(p_wdtT[:, :], p_wdtTf[:, :]))

        @block.scalar
        def _(sc):
            _w(sc, (sX, sX.c))
            sA.tick(sc.activation(p_tmp[:, 0:G * N], p_A[:, :], AF.Exp))

        @block.vector
        def _(v):
            _w(v, (sA, sA.c))
            sV.tick(v.tensor_scalar_mul(p_A[:, :], p_tmp[:, 0:G * N], -1.0))

        mes = ExitStack()
        ME = mes.enter_context
        m_xw = ME(nc.sbuf_tensor("m_xw", [128, 8 * DC], F32))
        m_x = m_xw[:, 0:2 * DC]
        m_ln = ME(nc.sbuf_tensor("m_ln", [128, DC], F32))
        m_xbuf = ME(nc.sbuf_tensor("m_xbuf", [128, G * (LC + 3)], F32))
        m_u = ME(nc.sbuf_tensor("m_u", [128, 3 * GL], F32))
        m_ur = ME(nc.sbuf_tensor("m_ur", [128, GL], F32R))
        m_du = ME(nc.sbuf_tensor("m_du", [128, GL], F32))
        m_xpd = ME(nc.sbuf_tensor("m_xpd", [128, LC], F32))
        m_xpbc = ME(nc.sbuf_tensor("m_xpbc", [2 * N, LC], F32))
        m_dtin = ME(nc.sbuf_tensor("m_dtin", [128, 2 * LC], F32))
        m_dtr = ME(nc.sbuf_tensor("m_dtr", [128, 2 * LC], F32R))
        m_ta = ME(nc.sbuf_tensor("m_ta", [128, G * LC], F32))
        m_tb = ME(nc.sbuf_tensor("m_tb", [128, G * LC], F32))
        m_del = ME(nc.sbuf_tensor("m_del", [128, 2 * GL], F32))
        m_dA = ME(nc.sbuf_tensor("m_dA", [128, 2 * GL], F32))
        m_dBu = ME(nc.sbuf_tensor("m_dBu", [128, 2 * GL], F32))
        m_h = ME(nc.sbuf_tensor("m_h", [128, 2 * GL], F32))
        m_t = ME(nc.sbuf_tensor("m_t", [128, 2 * GL], F32))
        m_y = ME(nc.sbuf_tensor("m_y", [128, GL], F32))
        m_Bs = ME(nc.sbuf_tensor("m_Bs", [128, 2 * LC], F32))
        m_Cs = ME(nc.sbuf_tensor("m_Cs", [128, 2 * LC], F32))
        m_yr = ME(nc.sbuf_tensor("m_yr", [128, 2 * DC], F32))

        RW = MT * 128
        f_pool = ME(nc.sbuf_tensor("f_pool", [128, 2 * D], F32))
        f_yt = f_pool[:, 0:D]
        f_xr = f_pool[:, D:2 * D]
        f_h2r = f_pool.ap().bitcast(BF16)
        f_t = ME(nc.sbuf_tensor("f_t", [128, D], F32))
        f_h1 = ME(nc.sbuf_tensor("f_h1", [128, MB1 * RW], BF16))
        f_h2T = f_h1
        f_x2T = ME(nc.sbuf_tensor("f_x2T", [128, KT1 * RW], BF16))
        f_o3b = f_x2T.ap().bitcast(F32)
        f_ws1 = ME(nc.sbuf_tensor("f_ws1", [128, 2 * 4096], BF16))
        f_ws23 = ME(nc.sbuf_tensor("f_ws23", [128, 8 * 512], BF16))
        f_bn = ME(nc.sbuf_tensor("f_bn", [128, 6 * 8], F32))
        f_sg = ME(nc.sbuf_tensor("f_sg", [128, 16], F32))

        with mes:
            ma_ld, ma_sum, ma_red = {}, {}, {}
            for i in range(NTL // 4):
                sl4 = i % 2

                @block.sync
                def _(sync, i=i, sl4=sl4):
                    b = (i * 512) // LSEQ
                    l0 = (i * 512) % LSEQ
                    _w(sync, (sV, ma_red.get(i - 2, 0)))
                    sX.tick(sync.dma_start(
                        out=m_xw[:, sl4 * 4 * DC:(sl4 + 1) * 4 * DC]
                            .rearrange("p (q c) -> p q c", q=4),
                        in_=xsl[b, l0:l0 + 512, :].rearrange("(q p) c -> p q c", p=128)), 16)
                    _w(sync, (sX, sX.c))
                    ma_ld[i] = sX.c

                @block.vector
                def _(v, i=i, sl4=sl4):
                    _w(v, (sX, ma_ld[i]))
                    for q in range(4):
                        jt = i * 4 + q
                        sV.tick(v.reduce_sum(
                            p_sum[:, jt:jt + 1],
                            m_xw[:, (sl4 * 4 + q) * DC:(sl4 * 4 + q + 1) * DC], AX.X))
                    ma_sum[i] = sV.c

                @block.scalar
                def _(sc, i=i, sl4=sl4):
                    _w(sc, (sV, ma_sum[i]))
                    sA.tick(sc.activation(m_xw[:, sl4 * 4 * DC:(sl4 + 1) * 4 * DC],
                                          m_xw[:, sl4 * 4 * DC:(sl4 + 1) * 4 * DC],
                                          AF.Square))

                @block.vector
                def _(v, i=i, sl4=sl4):
                    _w(v, (sA, sA.c))
                    for q in range(4):
                        jt = i * 4 + q
                        sV.tick(v.reduce_sum(
                            p_sq[:, jt:jt + 1],
                            m_xw[:, (sl4 * 4 + q) * DC:(sl4 * 4 + q + 1) * DC], AX.X))
                    ma_red[i] = sV.c

            @block.sync
            def _(sync):
                _w(sync, (sV, sV.c))
                sX.tick(sync.dma_start(out=statp.ap()[:, 0:NTL], in_=p_sum[:, :]), 16)
                sX.tick(sync.dma_start(out=statp.ap()[:, NTL:2 * NTL], in_=p_sq[:, :]), 16)
                _w(sync, (sX, sX.c))

            @block.gpsimd
            def _(g):
                _w(g, (sX, sX.c))
                sC.tick(g.collective_compute(
                    "AllReduce", ALU.add, replica_groups=rg,
                    ins=[statp.ap().opt()], outs=[statf.ap().opt()]), 1)

            @block.sync
            def _(sync):
                _w(sync, (sC, sC.c))
                sX.tick(sync.dma_start(out=p_sum[:, :], in_=statf.ap()[:, 0:NTL]), 16)
                sX.tick(sync.dma_start(out=p_sq[:, :], in_=statf.ap()[:, NTL:2 * NTL]), 16)

            @block.vector
            def _(v):
                _w(v, (sX, sX.c))
                sV.tick(v.tensor_scalar_mul(p_nm[:, :], p_sum[:, :], -1.0 / D))
                _w(v, (sV, sV.c))
                sV.tick(v.tensor_mul(p_tmp[:, :], p_nm[:, :], p_nm[:, :]))
                _w(v, (sV, sV.c))
                sV.tick(v.tensor_scalar(p_rs[:, :], p_sq[:, :], 1.0 / D, None, ALU.mult))
                _w(v, (sV, sV.c))
                sV.tick(v.tensor_sub(p_rs[:, :], p_rs[:, :], p_tmp[:, :]))

            @block.scalar
            def _(sc):
                _w(sc, (sV, sV.c))
                sA.tick(sc.activation(p_tmp[:, :], p_rs[:, :], AF.Ln, bias=p_eps[:, 0:1]))
                _w(sc, (sA, sA.c))
                sA.tick(sc.activation(p_rs[:, :], p_tmp[:, :], AF.Exp, scale=-0.5))
            STAT_V, STAT_A = sV.c, sA.c

            arx_done = {}
            a2a_done = {}
            xpd_dma = {}
            xpj_evac_last = [0]
            xbuf_evac_last = [0]
            lnc, B1tr, xld, xbe = {}, {}, {}, {}
            conv_tick = {}
            ufree = {}
            carry = {"Bev": [0, 0], "ym": [0, 0], "scd": [0, 0], "dbu": 0}
            yfin_prev = [0]


            def emit_B1(t):
                b, tc = t // NCH, t % NCH
                l0 = tc * LC
                par = t % 2
                uoff = (t % 3) * GL

                for j in range(TPC):
                    jt = b * (LSEQ // 128) + (l0 // 128) + j
                    xpar = j % 2
                    xa = m_x[:, xpar * DC:(xpar + 1) * DC]

                    @block.sync
                    def _(sync, xa=xa, b=b, l0=l0, j=j, t=t):
                        _w(sync, (sV, lnc.get((t, j - 2),
                                              lnc.get((t - 1, TPC - 2 + j), 0))))
                        if t < 2:
                            _w(sync, (sV, STAT_V), (sA, STAT_A))
                        sX.tick(sync.dma_start(
                            out=xa, in_=xsl[b, l0 + 128 * j:l0 + 128 * (j + 1), :]), 16)
                        _w(sync, (sX, sX.c))
                        xld[(t, j)] = sX.c

                    @block.vector
                    def _(v, xa=xa, jt=jt, j=j, t=t):
                        _w(v, (sX, xld[(t, j)]), (sA, STAT_A), (sV, STAT_V))
                        _w(v, (sP, B1tr.get((t, j - 1),
                                            B1tr.get((t - 1, TPC - 1), 0))))
                        sV.tick(v.tensor_scalar(m_ln[:, :], xa, p_nm[:, jt:jt + 1],
                                                p_rs[:, jt:jt + 1], ALU.add, ALU.mult))
                        lnc[(t, j)] = sV.c

                    @block.tensor
                    def _(tn, j=j, t=t):
                        _w(tn, (sV, lnc[(t, j)]))
                        if j == 0:
                            _w(tn, (sA, yev.get((t - 3, TPC - 1), 0)))
                        else:
                            _w(tn, (sA, xbe[(t, j - 1)]))
                        for g in range(G):
                            sP.tick(tn.transpose(pTg(g), m_ln[:, 128 * g:128 * (g + 1)],
                                                 p_id[:, :]))
                        B1tr[(t, j)] = sP.c

                    @block.scalar
                    def _(sc, j=j, t=t):
                        _w(sc, (sP, B1tr[(t, j)]))
                        if t >= 1 and j == 0:
                            _w(sc, (sV, conv_tick[t - 1]))
                        for g in range(G):
                            o = g * (LC + 3) + 3 + 128 * j
                            sA.tick(sc.copy(m_xbuf[:, o:o + 128], pTg(g)))
                        xbe[(t, j)] = sA.c
                    xbuf_evac_last[0] = sA.c

                @block.vector
                def _(v, t=t, par=par, uoff=uoff, tc=tc):
                    _w(v, (sA, sA.c))
                    if t == 0:
                        for g in range(G):
                            sV.tick(v.memset(
                                m_xbuf[:, g * (LC + 3):g * (LC + 3) + 3], 0.0))
                        _w(v, (sV, sV.c))
                    if t >= 3:
                        _w(v, (sV, ufree[t - 3]))
                    for g in range(G):
                        xb = m_xbuf[:, g * (LC + 3):(g + 1) * (LC + 3)]
                        ug = m_u[:, uoff + g * LC:uoff + (g + 1) * LC]
                        sV.tick(v.tensor_scalar(ug, xb[:, 0:LC], p_cw[:, g * K:g * K + 1],
                                                p_cb[:, g:g + 1], ALU.mult, ALU.add))
                        for k in range(1, K):
                            _w(v, (sV, sV.c))
                            sV.tick(v.scalar_tensor_tensor(
                                ug, xb[:, k:k + LC], p_cw[:, g * K + k:g * K + k + 1],
                                ug, ALU.mult, ALU.add))
                    for g in range(G):
                        _w(v, (sV, sV.c))
                        xb = m_xbuf[:, g * (LC + 3):(g + 1) * (LC + 3)]
                        if tc == NCH - 1:
                            sV.tick(v.memset(xb[:, 0:3], 0.0))
                        else:
                            sV.tick(v.tensor_copy(xb[:, 0:3], xb[:, LC:LC + 3]))
                    _w(v, (sV, sV.c))
                    sV.tick(v.tensor_copy(m_ur[:, 0:GL], m_u[:, uoff:uoff + GL]))
                conv_tick[t] = sV.c

                @block.tensor
                def _(tn, uoff=uoff):
                    _w(tn, (sV, sV.c), (sA, sA.c))
                    for g in range(G):
                        sP.tick(tn.matmul(pP[:, :], p_wxpT[:, g * FT:g * FT + 128],
                                          m_ur[:, g * LC:(g + 1) * LC],
                                          start=(g == 0), stop=(g == G - 1)))

                @block.scalar
                def _(sc, t=t):
                    _w(sc, (sP, sP.c))
                    if t >= 1:
                        _w(sc, (sX, xpd_dma[t - 1]))
                    sA.tick(sc.copy(m_xpd[:, :], pP[:, :]))

                @block.tensor
                def _(tn, uoff=uoff):
                    _w(tn, (sA, sA.c))
                    for g in range(G):
                        sP.tick(tn.matmul(pP[0:2 * N, :],
                                          p_wxpT[:, g * FT + 128:(g + 1) * FT],
                                          m_ur[:, g * LC:(g + 1) * LC],
                                          start=(g == 0), stop=(g == G - 1)))

                @block.scalar
                def _(sc):
                    _w(sc, (sP, sP.c))
                    sA.tick(sc.copy(m_xpbc[:, :], pP[0:2 * N, :]))
                xpj_evac_last[0] = sA.c

                @block.sync
                def _(sync, t=t):
                    _w(sync, (sA, sA.c))
                    sX.tick(sync.dma_start(out=xdp.ap()[t, 0:128, :], in_=m_xpd[:, :]), 16)
                    sX.tick(sync.dma_start(out=xdp.ap()[t, 128:FT, :], in_=m_xpbc[:, :]), 16)
                    _w(sync, (sX, sX.c))
                xpd_dma[t] = sX.c

            def emit_AR(t):
                @block.gpsimd
                def _(g, t=t):
                    _w(g, (sX, max(xpd_dma[t], xpd_dma[t + 1])))
                    sC.tick(g.collective_compute(
                        "AllReduce", ALU.add, replica_groups=rg,
                        ins=[xdp.ap()[t:t + 2].opt()], outs=[xdf.ap()[t:t + 2].opt()]), 1)
                arx_done[t] = arx_done[t + 1] = sC.c

            D_rnd, D_dtmm, D_spact, D_du, D_spv = {}, {}, {}, {}, {}
            D_exp, D_scan, D_st, D_dbu = {}, {}, {}, {}
            D_yC, D_add, D_yfin, D_ytrend, D_ydend = {}, {}, {}, {}, {}
            bcB, bcC, ytr, yev, D_ydma = {}, {}, {}, {}, {}

            def pDT(g):
                return pP if g == 0 else pT

            def emit_B2(t):
                b, tc = t // NCH, t % NCH
                cpar = t % 2
                co = cpar * LC
                cog = cpar * GL
                uoff = (t % 3) * GL
                kpass = t // CHG

                @block.gpsimd
                def _(g_, t=t):
                    for tt in (t + 1, t + 2):
                        if (tt % 2 == 0 and tt in xpd_dma and tt + 1 in xpd_dma
                                and tt not in arx_done):
                            _w(g_, (sX, max(xpd_dma[tt], xpd_dma[tt + 1])))
                            sC.tick(g_.collective_compute(
                                "AllReduce", ALU.add, replica_groups=rg,
                                ins=[xdp.ap()[tt:tt + 2].opt()],
                                outs=[xdf.ap()[tt:tt + 2].opt()]), 1)
                            arx_done[tt] = arx_done[tt + 1] = sC.c

                @block.sync
                def _(sync, t=t, co=co):
                    _w(sync, (sC, arx_done[t]))
                    _w(sync, (sV, D_rnd.get(t - 2, 0)))
                    sX.tick(sync.dma_start(out=m_dtin[:, co:co + LC],
                                           in_=xdf.ap()[t, 0:128, :]), 16)
                    _w(sync, (sX, sX.c))
                xdf_dma = sX.c

                @block.vector
                def _(v, t=t, co=co):
                    _w(v, (sX, xdf_dma), (sP, D_dtmm.get(t - 2, 0)))
                    sV.tick(v.tensor_copy(m_dtr[:, co:co + LC], m_dtin[:, co:co + LC]))
                D_rnd[t] = sV.c

                @block.scalar
                def _(sync, t=t):
                    _w(sync, (sC, arx_done[t]))
                    _w(sync, (sV, D_dbu.get((t - 1, N - 1), 0)))
                    _w(sync, (sV, D_yC.get((t - 1, N - 1), 0)))
                    sbx = semB[bcnt[0] % 2]
                    bcnt[0] += 1
                    sbx.tick(sync.dma_start(
                        out=m_Bs[:, :].rearrange("p (i l) -> p i l", i=2),
                        in_=xdf.ap()[t, 128:130, :].unsqueeze(0)
                            .broadcast_to([128, 2, LC])), 16)
                    bcB[(t, 0)] = bcB[(t, 1)] = (sbx, sbx.c)
                    scx = semC[ccnt[0] % 2]
                    ccnt[0] += 1
                    scx.tick(sync.dma_start(
                        out=m_Cs[:, :].rearrange("p (i l) -> p i l", i=2),
                        in_=xdf.ap()[t, 128 + N:130 + N, :].unsqueeze(0)
                            .broadcast_to([128, 2, LC])), 16)
                    bcC[(t, 0)] = bcC[(t, 1)] = (scx, scx.c)

                @block.tensor
                def _(tn, t=t, co=co):
                    _w(tn, (sV, D_rnd[t]), (sA, xpj_evac_last[0]))
                    for g in range(G):
                        sP.tick(tn.matmul(pDT(g)[:, :], p_wdtT[:, 128 * g:128 * (g + 1)],
                                          m_dtr[:, co:co + LC], start=True, stop=True))
                D_dtmm[t] = sP.c

                @block.scalar
                def _(sc, t=t, cog=cog):
                    _w(sc, (sP, D_dtmm[t]))
                    _w(sc, (sA, D_exp.get((t - 2, N - 1), 0)))
                    _w(sc, (sV, D_du.get(t - 2, 0)))
                    _w(sc, (sV, D_spv.get(t - 1, 0)))
                    for g in range(G):
                        sA.tick(sc.activation(m_ta[:, g * LC:(g + 1) * LC], pDT(g)[:, :],
                                              AF.Abs, bias=p_dtb[:, g:g + 1]))
                    for g in range(G):
                        sA.tick(sc.activation(m_del[:, cog + g * LC:cog + (g + 1) * LC],
                                              pDT(g)[:, :], AF.Relu,
                                              bias=p_dtb[:, g:g + 1]))
                    for g in range(G):
                        sA.tick(sc.activation(m_tb[:, g * LC:(g + 1) * LC],
                                              m_ta[:, g * LC:(g + 1) * LC],
                                              AF.Exp, scale=-1.0))
                    for g in range(G):
                        sA.tick(sc.activation(m_ta[:, g * LC:(g + 1) * LC],
                                              m_tb[:, g * LC:(g + 1) * LC],
                                              AF.Ln, bias=1.0))
                D_spact[t] = sA.c

                @block.vector
                def _(v, t=t, tc=tc, cog=cog, uoff=uoff):
                    _w(v, (sA, D_spact[t]))
                    for g in range(G):
                        dg = m_del[:, cog + g * LC:cog + (g + 1) * LC]
                        sV.tick(v.tensor_add(dg, dg, m_ta[:, g * LC:(g + 1) * LC]))
                    D_spv[t] = sV.c
                    _w(v, (sV, sV.c))
                    sV.tick(v.tensor_mul(m_du[:, 0:GL], m_del[:, cog:cog + GL],
                                         m_u[:, uoff:uoff + GL]))
                    D_du[t] = sV.c
                    _w(v, (sV, sV.c))
                    if tc == 0:
                        sV.tick(v.memset(p_st[:, :], 0.0))
                    for n in (0, 1):
                        npar = n % 2
                        _w(v, bcB[(t, n)])
                        du3 = m_du[:, 0:GL].rearrange("p (g l) -> p g l", g=G)
                        dBu3 = m_dBu[:, npar * GL:(npar + 1) * GL].rearrange(
                            "p (g l) -> p g l", g=G)
                        B3 = m_Bs[:, npar * LC:(npar + 1) * LC].unsqueeze(1).broadcast_to(
                            [128, G, LC])
                        sV.tick(v.tensor_tensor(dBu3, du3, B3, ALU.mult))
                        D_dbu[(t, n)] = sV.c

                @block.scalar
                def _(sc, t=t, cog=cog):
                    for n in (0, 1):
                        npar = n % 2
                        _w(sc, (sV, D_scan.get((t - 1, N - 2 + n), 0)))
                        for g in range(G):
                            sA.tick(sc.activation(
                                m_dA[:, npar * GL + g * LC:npar * GL + (g + 1) * LC],
                                m_del[:, cog + g * LC:cog + (g + 1) * LC], AF.Exp,
                                scale=p_A[:, g * N + n:g * N + n + 1]))
                        D_exp[(t, n)] = sA.c

                for n in range(N):
                    npar = n % 2
                    hoff = npar * GL

                    @block.vector
                    def _(v, t=t, n=n, npar=npar, hoff=hoff, tc=tc):
                        _w(v, (sA, D_exp[(t, n)]))
                        for g in range(G):
                            sV.tick(v.tensor_tensor_scan(
                                m_h[:, hoff + g * LC:hoff + (g + 1) * LC],
                                m_dA[:, hoff + g * LC:hoff + (g + 1) * LC],
                                m_dBu[:, hoff + g * LC:hoff + (g + 1) * LC],
                                p_st[:, g * N + n:g * N + n + 1],
                                ALU.mult, ALU.add))
                        D_scan[(t, n)] = sV.c

                    if n + 1 < N:
                        @block.vector
                        def _(v, t=t, n=n, npar=npar):
                            np1 = (n + 1) % 2
                            _w(v, bcB[(t, n + 1)])
                            du3 = m_du[:, 0:GL].rearrange("p (g l) -> p g l", g=G)
                            dBu3 = m_dBu[:, np1 * GL:(np1 + 1) * GL].rearrange(
                                "p (g l) -> p g l", g=G)
                            B3 = m_Bs[:, np1 * LC:(np1 + 1) * LC].unsqueeze(1).broadcast_to(
                                [128, G, LC])
                            sV.tick(v.tensor_tensor(dBu3, du3, B3, ALU.mult))
                            D_dbu[(t, n + 1)] = sV.c

                    if n + 2 < N:
                        @block.scalar
                        def _(sc, t=t, n=n, npar=npar, cog=cog):
                            _w(sc, (sV, D_scan[(t, n)]))
                            for g in range(G):
                                sA.tick(sc.activation(
                                    m_dA[:, npar * GL + g * LC:npar * GL + (g + 1) * LC],
                                    m_del[:, cog + g * LC:cog + (g + 1) * LC], AF.Exp,
                                    scale=p_A[:, g * N + n + 2:g * N + n + 3]))
                            D_exp[(t, n + 2)] = sA.c

                    @block.vector
                    def _(v, t=t, n=n, npar=npar, hoff=hoff):
                        if n == N - 1:
                            _w(v, (sV, sV.c))
                        hlast = bass.AP(m_h, hoff + LC - 1, [[2 * GL, 128], [LC, G], [1, 1]])
                        stv = bass.AP(p_st, n, [[G * N, 128], [N, G], [1, 1]])
                        sV.tick(v.tensor_copy(stv, hlast))
                        _w(v, bcC[(t, n)])
                        h3 = m_h[:, hoff:hoff + GL].rearrange("p (g l) -> p g l", g=G)
                        t3 = m_t[:, hoff:hoff + GL].rearrange("p (g l) -> p g l", g=G)
                        C3 = m_Cs[:, npar * LC:(npar + 1) * LC].unsqueeze(1).broadcast_to(
                            [128, G, LC])
                        sV.tick(v.tensor_tensor(t3, h3, C3, ALU.mult))
                        D_yC[(t, n)] = sV.c
                        _w(v, (sV, sV.c))
                        if n == 0:
                            _w(v, (sP, D_ytrend.get(t - 1, 0)))
                            sV.tick(v.tensor_copy(m_y[:, :], m_t[:, hoff:hoff + GL]))
                        else:
                            sV.tick(v.tensor_add(m_y[:, :], m_y[:, :],
                                                 m_t[:, hoff:hoff + GL]))
                        D_add[(t, n)] = sV.c

                    if n + 2 < N:
                        @block.scalar
                        def _(sync, t=t, n=n, npar=npar):
                            if n % 2 == 0:
                                _w(sync, (sV, D_dbu[(t, n + 1)]))
                                sbx = semB[bcnt[0] % 2]
                                bcnt[0] += 1
                                sbx.tick(sync.dma_start(
                                    out=m_Bs[:, :].rearrange("p (i l) -> p i l", i=2),
                                    in_=xdf.ap()[t, 130 + n:132 + n, :].unsqueeze(0)
                                        .broadcast_to([128, 2, LC])), 16)
                                bcB[(t, n + 2)] = bcB[(t, n + 3)] = (sbx, sbx.c)
                            _w(sync, (sV, D_yC[(t, n)]))
                            scx = semC[ccnt[0] % 2]
                            ccnt[0] += 1
                            scx.tick(sync.dma_start(
                                out=m_Cs[:, npar * LC:(npar + 1) * LC],
                                in_=xdf.ap()[t, 128 + N + n + 2:129 + N + n + 2, :]
                                    .broadcast_to([128, LC])), 16)
                            bcC[(t, n + 2)] = (scx, scx.c)

                @block.vector
                def _(v, t=t, uoff=uoff):
                    _w(v, (sV, D_add[(t, N - 1)]))
                    for g in range(G):
                        sV.tick(v.scalar_tensor_tensor(
                            m_y[:, g * LC:(g + 1) * LC],
                            m_u[:, uoff + g * LC:uoff + (g + 1) * LC],
                            p_D[:, g:g + 1],
                            m_y[:, g * LC:(g + 1) * LC], ALU.mult, ALU.add))
                D_yfin[t] = sV.c
                ufree[t] = sV.c

            def emit_B2_tail(t):
                kpass = t // CHG
                for j in range(TPC):
                    jp = j % 2

                    @block.tensor
                    def _(tn, t=t, j=j):
                        if j == 0:
                            _w(tn, (sV, D_yfin[t]), (sA, D_spact[t]))
                            _w(tn, (sA, xbuf_evac_last[0]))
                        else:
                            _w(tn, (sA, yev[(t, j - 1)]))
                        for g in range(G):
                            ins = tn.transpose(
                                pTg(g), m_y[:, g * LC + 128 * j:g * LC + 128 * (j + 1)],
                                p_id[:, :])
                        sP.tick(ins)
                        ytr[(t, j)] = sP.c

                    @block.scalar
                    def _(sc, t=t, j=j, jp=jp):
                        _w(sc, (sP, ytr[(t, j)]))
                        if j < 2:
                            _w(sc, (sX, D_ydma.get((t - 1, j + 2), 0)))
                        else:
                            _w(sc, (sX, D_ydma[(t, j - 2)]))
                        for g in range(G):
                            sA.tick(sc.copy(m_yr[:, jp * DC + 128 * g:jp * DC + 128 * (g + 1)],
                                            pTg(g)))
                        yev[(t, j)] = sA.c

                    @block.sync
                    def _(sync, t=t, j=j, jp=jp, kpass=kpass):
                        _w(sync, (sA, yev[(t, j)]))
                        Jt = (t % CHG) * TPC + j
                        shard, slot = Jt % NCORES, Jt // NCORES
                        r0 = shard * (MT * 128) + slot * 128
                        sX.tick(sync.dma_start(
                            out=ya2a_i.ap()[kpass, r0:r0 + 128, :],
                            in_=m_yr[:, jp * DC:(jp + 1) * DC]), 16)
                        _w(sync, (sX, sX.c))
                        D_ydma[(t, j)] = sX.c
                D_ytrend[t] = ytr[(t, TPC - 1)]
                D_ydend[t] = D_ydma[(t, TPC - 1)]

                if (t + 1) % CHG == 0:
                    @block.gpsimd
                    def _(g_, t=t, kpass=kpass):
                        _w(g_, (sX, D_ydend[t]))
                        with allow_ici_all_to_all():
                            sC.tick(g_.collective_compute(
                                "AllToAll", ALU.bypass, replica_groups=rg,
                                ins=[ya2a_i.ap()[kpass].opt()],
                                outs=[ya2a_o.ap()[kpass].opt()]), 1)
                        a2a_done[kpass] = sC.c

            HS = H // 512
            DS = D // 512
            pW = [pB0, pB1, pC0, pC1]
            pFT = pF0
            fst = {}

            def emit_ln_rowmajor(srcap):
                @block.vector
                def _(v):
                    _w(v, (sV, sV.c))
                    for i in range(DB):
                        sV.tick(v.bn_stats(f_bn[:, 6 * i:6 * (i + 1)],
                                           srcap[:, 512 * i:512 * (i + 1)]))
                    _w(v, (sV, sV.c))
                    sV.tick(v.bn_aggr(f_sg[:, 0:2], f_bn[:, 0:6 * DB]))

                @block.scalar
                def _(sc):
                    _w(sc, (sV, sV.c))
                    sA.tick(sc.activation(f_sg[:, 4:5], f_sg[:, 1:2], AF.Ln,
                                          bias=p_eps[:, 0:1]))
                    _w(sc, (sA, sA.c))
                    sA.tick(sc.activation(f_sg[:, 3:4], f_sg[:, 4:5], AF.Exp, scale=-0.5))

                @block.vector
                def _(v):
                    _w(v, (sA, sA.c), (sV, sV.c))
                    sV.tick(v.tensor_scalar_mul(f_sg[:, 2:3], f_sg[:, 0:1], -1.0))
                    _w(v, (sV, sV.c))
                    sV.tick(v.tensor_mul(f_sg[:, 5:6], f_sg[:, 2:3], f_sg[:, 3:4]))

            def emit_prep(kp, m):
                s = fst.setdefault(kp, {})

                @block.sync
                def _(sync, kp=kp, m=m, s=s):
                    _w(sync, (sC, a2a_done[kp]))
                    _w(sync, (sP, fst.get(kp - 1, {}).get('h2T_end', 0)))
                    _w(sync, (sV, fst.get(kp - 1, {}).get('o3v1', 0)))
                    if m == 1:
                        _w(sync, (sV, s['pool_free0']))
                    for i in range(NCORES):
                        r0 = i * RW + m * 128
                        sX.tick(sync.dma_start(out=f_yt[:, i * DC:(i + 1) * DC],
                                               in_=ya2a_o.ap()[kp, r0:r0 + 128, :]), 16)
                    sX.tick(sync.dma_start(out=f_xr[:, :], in_=xrows[kp * MT + m]), 16)
                    _w(sync, (sX, sX.c))
                yt_dma = sX.c

                @block.vector
                def _(v, yt_dma=yt_dma):
                    _w(v, (sX, yt_dma))
                emit_ln_rowmajor(f_yt)

                @block.vector
                def _(v, kp=kp, m=m):
                    _w(v, (sV, sV.c))
                    if m == 0:
                        _w(v, (sX, fst.get(kp - 1, {}).get('out_dma', 0)))
                    sV.tick(v.tensor_scalar(f_t[:, :], f_yt[:, :], f_sg[:, 3:4],
                                            f_sg[:, 5:6], ALU.mult, ALU.add))
                    _w(v, (sV, sV.c))
                    sV.tick(v.tensor_add(f_yt[:, :], f_xr[:, :], f_t[:, :]))
                xm_tick = sV.c

                @block.sync
                def _(sync, kp=kp, m=m, xm_tick=xm_tick, s=s):
                    _w(sync, (sV, xm_tick))
                    _w(sync, (sX, fst.get(kp - 2, {}).get('xm_ld', 0)))
                    sX.tick(sync.dma_start(out=xmid_sp.ap()[kp % 2, m], in_=f_yt[:, :]), 16)
                    _w(sync, (sX, sX.c))
                    s['spill%d' % m] = sX.c

                emit_ln_rowmajor(f_yt)

                @block.vector
                def _(v, m=m):
                    _w(v, (sV, sV.c))
                    f_x2 = f_t.ap().bitcast(BF16)[:, 0:D]
                    sV.tick(v.tensor_scalar(f_x2, f_yt[:, :], f_sg[:, 3:4],
                                            f_sg[:, 5:6], ALU.mult, ALU.add))
                x2_tick = sV.c
                s['pool_free%d' % m] = x2_tick

                for kt2 in range(KT1 // 4):
                    @block.tensor
                    def _(tn, m=m, kt2=kt2, x2_tick=x2_tick):
                        _w(tn, (sV, x2_tick), (sA, sA.c))
                        for q in range(4):
                            kt = kt2 * 4 + q
                            ins = tn.transpose(
                                pFT.ap().bitcast(BF16)[:, 256 * q:256 * q + 128],
                                f_t.ap().bitcast(BF16)[:, 128 * kt:128 * (kt + 1)],
                                p_idb[:, :])
                        sP.tick(ins)
                    tr4 = sP.c

                    @block.scalar
                    def _(sc, kp=kp, m=m, kt2=kt2, tr4=tr4):
                        _w(sc, (sP, tr4))
                        if m == 0 and kt2 == 0:
                            _w(sc, (sX, fst.get(kp - 1, {}).get('out_dma', 0)))
                        dst = bass.AP(f_x2T, kt2 * 4 * RW + 128 * m,
                                      [[KT1 * RW, 128], [RW, 4], [1, 128]])
                        src = pFT.ap().bitcast(BF16)[:, 0:1024].rearrange(
                            "p (q j) -> p q j", q=4)[:, :, 0:128]
                        sA.tick(sc.copy(dst, src))
                s['x2Ttr%d' % m] = sP.c
                s['x2T%d' % m] = sA.c

            def emit_W1(kp, half):
                s = fst[kp]
                evt = s.setdefault('w1evt', {})
                sl = s.setdefault('w1slab', {})
                for mg in range(half * (NS1 // 2), (half + 1) * (NS1 // 2)):
                    wpar = mg % 2

                    @block.sync
                    def _(sync, kp=kp, mg=mg, wpar=wpar, s=s):
                        _w(sync, (sP, s.get('w1used%d' % (mg - 2), 0)))
                        if mg < 2:
                            _w(sync, (sP, fst.get(kp - 1, {}).get('w1end', 0)))
                        if kp == 0:
                            _w(sync, (sG, CW1[mg // 4]))
                        swx = semW[wcnt[0] % 4]
                        wcnt[0] += 1
                        swx.tick(sync.dma_start(
                            out=f_ws1[:, wpar * 4096:(wpar + 1) * 4096],
                            in_=wt1b.ap()[mg]), 16)
                        sl[mg] = (swx, swx.c)

                    for mb in range(2):
                        mi = mg * 2 + mb
                        pF = pW[mi % 2]

                        @block.tensor
                        def _(tn, kp=kp, mg=mg, mb=mb, mi=mi, pF=pF, wpar=wpar, s=s):
                            _w(tn, sl[mg])
                            _w(tn, (sA, max(s['x2T0'], s['x2T1'], evt.get(mi - 2, 0))))
                            if mi == 0:
                                _w(tn, (sV, fst.get(kp - 1, {}).get('w3stats', 0)))
                                _w(tn, (sA, fst.get(kp - 1, {}).get('w3evac', 0)))
                            for kt in range(KT1):
                                sP.tick(tn.matmul(
                                    pF[:, 0:RW],
                                    f_ws1[:, wpar * 4096 + (kt * 2 + mb) * 128:
                                          wpar * 4096 + (kt * 2 + mb + 1) * 128],
                                    f_x2T[:, kt * RW:(kt + 1) * RW],
                                    start=(kt == 0), stop=(kt == KT1 - 1),
                                    skip_group_check=True))
                        mm_done = sP.c

                        @block.scalar
                        def _(sc, kp=kp, mi=mi, pF=pF, mm_done=mm_done):
                            _w(sc, (sP, mm_done))
                            if mi == 0:
                                _w(sc, (sP, fst.get(kp - 1, {}).get('w3mm_end', 0)))
                            sA.tick(sc.activation(f_h1[:, mi * RW:(mi + 1) * RW],
                                                  pF[:, 0:RW], AF.Relu,
                                                  bias=p_b1[:, mi:mi + 1]))
                            evt[mi] = sA.c
                    s['w1used%d' % mg] = sP.c
                if half == 1:
                    s['w1end'] = sP.c
                    s['w1evt_end'] = sA.c

            def emit_W2(kp, hp):
                s = fst[kp]
                BT = 2
                slab, mmb = {}, {}
                for bt in range(KTH // BT):
                    @block.sync
                    def _(sync, kp=kp, hp=hp, bt=bt, s=s):
                        _w(sync, (sP, mmb.get(bt - 2, 0)))
                        if bt < 2:
                            _w(sync, (sP, s.get('w2mm%d' % (hp - 1), 0)))
                            if hp == 0:
                                _w(sync, (sP, fst.get(kp - 1, {}).get('w3mm_end', 0)))
                        if kp == 0:
                            _w(sync, (sG, CW2[(hp * 64 + bt * 4) // 32]))
                        s0 = hp * 64 + bt * 4
                        swx = semW[wcnt[0] % 4]
                        wcnt[0] += 1
                        swx.tick(sync.dma_start(
                            out=f_ws23[:, (bt % 2) * 2048:(bt % 2) * 2048 + 2048]
                                .rearrange("p (i j) -> p i j", i=4),
                            in_=wt2b.ap()[s0:s0 + 4].rearrange("i p j -> p i j")), 16)
                        slab[bt] = (swx, swx.c)

                    @block.tensor
                    def _(tn, kp=kp, hp=hp, bt=bt, s=s):
                        _w(tn, slab[bt])
                        if bt == 0:
                            _w(tn, (sA, s['w1evt_end']))
                            _w(tn, (sA, s.get('w2evac%d' % (hp - 1), 0)))
                        for kt in range(bt * BT, (bt + 1) * BT):
                            for m in range(2):
                                for sx2 in range(2):
                                    slot = (kt % (2 * BT)) * 2 + sx2
                                    sP.tick(tn.matmul(
                                        pW[2 * m + sx2][:, 0:512],
                                        f_h1[:, kt * RW + m * 128:kt * RW + (m + 1) * 128],
                                        f_ws23[:, slot * 512:(slot + 1) * 512],
                                        start=(kt == 0), stop=False,
                                        skip_group_check=True))
                        mmb[bt] = sP.c

                @block.sync
                def _(sync, kp=kp, hp=hp, s=s):
                    _w(sync, (sP, s.get('bmm%d' % (hp - 1), 0)))
                    if kp == 0 and hp == 0:
                        _w(sync, (sG, CAST_B))
                    swx = semW[wcnt[0] % 4]
                    wcnt[0] += 1
                    swx.tick(sync.dma_start(
                        out=p_bs[0:1, :], in_=b2bd.ap()[0:1, hp * 1024:(hp + 1) * 1024]), 16)
                    s['bld%d' % hp] = (swx, swx.c)

                @block.tensor
                def _(tn, kp=kp, hp=hp, s=s):
                    _w(tn, s['bld%d' % hp])
                    for m in range(2):
                        for sx2 in range(2):
                            sP.tick(tn.matmul(pW[2 * m + sx2][:, 0:512], p_onesb[:, :],
                                              p_bs[0:1, sx2 * 512:(sx2 + 1) * 512],
                                              start=False, stop=True,
                                              skip_group_check=True))
                s['w2mm%d' % hp] = sP.c
                s['bmm%d' % hp] = sP.c

                @block.scalar
                def _(sc, kp=kp, hp=hp, s=s):
                    _w(sc, (sP, s['w2mm%d' % hp]))
                    if hp == 0:
                        _w(sc, (sV, s['pool_free1']))
                        _w(sc, (sX, s['spill1']))
                    for m in range(2):
                        for sx2 in range(2):
                            hs = hp * 2 + sx2
                            sA.tick(sc.activation(
                                f_h2r[:, m * 4096 + hs * 512:m * 4096 + (hs + 1) * 512],
                                pW[2 * m + sx2][:, 0:512], AF.Relu))
                s['w2evac%d' % hp] = sA.c

            def emit_h2T(kp):
                s = fst[kp]
                for m in range(2):
                    for q4 in range(8):
                        @block.tensor
                        def _(tn, kp=kp, m=m, q4=q4, s=s):
                            _w(tn, (sA, sA.c))
                            if m == 0 and q4 == 0:
                                _w(tn, (sA, s['w2evac3']))
                            for q in range(4):
                                ht = q4 * 4 + q
                                ins = tn.transpose(
                                    pFT.ap().bitcast(BF16)[:, 256 * q:256 * q + 128],
                                    f_h2r[:, m * 4096 + ht * 128:m * 4096 + (ht + 1) * 128],
                                    p_idb[:, :])
                            sP.tick(ins)
                        tr = sP.c

                        @block.scalar
                        def _(sc, kp=kp, m=m, q4=q4, tr=tr, s=s):
                            _w(sc, (sP, tr))
                            if m == 0 and q4 == 0:
                                _w(sc, (sP, s['w2mm3']))
                            dst = bass.AP(f_h1, q4 * 4 * RW + m * 128,
                                          [[MB1 * RW, 128], [RW, 4], [1, 128]])
                            src = pFT.ap().bitcast(BF16)[:, 0:1024].rearrange(
                                "p (q j) -> p q j", q=4)[:, :, 0:128]
                            sA.tick(sc.copy(dst, src))
                s['h2T_end'] = sP.c
                s['h2Tev_end'] = sA.c

            def emit_W3(kp):
                s = fst[kp]
                BT3 = 2
                slab, mmb = {}, {}
                for dsp in range(2):
                    for bt in range(KTH // BT3):
                        fb = dsp * (KTH // BT3) + bt

                        @block.sync
                        def _(sync, kp=kp, dsp=dsp, bt=bt, fb=fb, s=s):
                            _w(sync, (sP, mmb.get(fb - 2, 0)))
                            if fb < 2:
                                _w(sync, (sP, s['w2mm3']))
                            if kp == 0:
                                _w(sync, (sG, CW3[(dsp * 64 + bt * 4) // 32]))
                            s0 = dsp * 64 + bt * 4
                            swx = semW[wcnt[0] % 4]
                            wcnt[0] += 1
                            swx.tick(sync.dma_start(
                                out=f_ws23[:, (fb % 2) * 2048:(fb % 2) * 2048 + 2048]
                                    .rearrange("p (i j) -> p i j", i=4),
                                in_=wt3b.ap()[s0:s0 + 4].rearrange("i p j -> p i j")), 16)
                            slab[(dsp, bt)] = (swx, swx.c)

                        @block.tensor
                        def _(tn, kp=kp, dsp=dsp, bt=bt, fb=fb, s=s):
                            _w(tn, slab[(dsp, bt)])
                            if bt == 0:
                                _w(tn, (sA, s['h2Tev_end']))
                                if dsp == 0:
                                    _w(tn, (sA, s['w2evac3']))
                                else:
                                    _w(tn, (sA, s['w3evac0']))
                            for kt in range(bt * BT3, (bt + 1) * BT3):
                                for m in range(2):
                                    for sx2 in range(2):
                                        slot = (kt % (2 * BT3)) * 2 + sx2
                                        sP.tick(tn.matmul(
                                            pW[2 * m + sx2][:, 0:512],
                                            f_h2T[:, kt * RW + m * 128:kt * RW + (m + 1) * 128],
                                            f_ws23[:, slot * 512:(slot + 1) * 512],
                                            start=(kt == 0), stop=False,
                                            skip_group_check=True))
                            mmb[fb] = sP.c

                    @block.sync
                    def _(sync, kp=kp, dsp=dsp, s=s):
                        _w(sync, (sP, s['bmm3'] if dsp == 0 else s['w3mm0']))
                        swx = semW[wcnt[0] % 4]
                        wcnt[0] += 1
                        swx.tick(sync.dma_start(
                            out=p_bs[0:1, :],
                            in_=b3bd.ap()[0:1, dsp * 1024:(dsp + 1) * 1024]), 16)
                        s['b3ld%d' % dsp] = (swx, swx.c)

                    @block.tensor
                    def _(tn, kp=kp, dsp=dsp, s=s):
                        _w(tn, s['b3ld%d' % dsp])
                        for m in range(2):
                            for sx2 in range(2):
                                sP.tick(tn.matmul(pW[2 * m + sx2][:, 0:512], p_onesb[:, :],
                                                  p_bs[0:1, sx2 * 512:(sx2 + 1) * 512],
                                                  start=False, stop=True,
                                                  skip_group_check=True))
                    s['w3mm%d' % dsp] = sP.c

                    @block.scalar
                    def _(sc, kp=kp, dsp=dsp, s=s):
                        _w(sc, (sP, s['w3mm%d' % dsp]))
                        if dsp == 0:
                            _w(sc, (sP, s['x2Ttr1']))
                            _w(sc, (sP, s['w1end']))
                        for m in range(2):
                            stage = f_t if m == 0 else f_o3b
                            for sx2 in range(2):
                                ds = dsp * 2 + sx2
                                sA.tick(sc.copy(stage[:, ds * 512:(ds + 1) * 512],
                                                pW[2 * m + sx2][:, 0:512]))
                    s['w3evac%d' % dsp] = sA.c
                s['w3mm_end'] = s['w3mm1']
                s['w3evac'] = s['w3evac1']

            def emit_W3_fin(kp):
                s = fst[kp]

                @block.vector
                def _(v, kp=kp, s=s):
                    _w(v, (sA, s['w3evac1']))
                    for m in range(2):
                        stage = f_t if m == 0 else f_o3b
                        for ds in range(4):
                            sl8 = m * 4 + ds
                            sV.tick(v.bn_stats(f_bn[:, 6 * sl8:6 * (sl8 + 1)],
                                               stage[:, ds * 512:(ds + 1) * 512]))
                    _w(v, (sV, sV.c))
                    for m in range(2):
                        sV.tick(v.bn_aggr(f_sg[:, 8 * m:8 * m + 2],
                                          f_bn[:, 24 * m:24 * m + 24]))
                s['w3stats1'] = sV.c
                s['w3stats'] = sV.c
                ag_v = sV.c

                @block.scalar
                def _(sc, kp=kp, ag_v=ag_v, s=s):
                    _w(sc, (sA, s['w3evac1']), (sV, ag_v))
                    for m in range(2):
                        o = 8 * m
                        sA.tick(sc.activation(f_sg[:, o + 4:o + 5], f_sg[:, o + 1:o + 2],
                                              AF.Ln, bias=p_eps[:, 0:1]))
                        _w(sc, (sA, sA.c))
                        sA.tick(sc.activation(f_sg[:, o + 3:o + 4], f_sg[:, o + 4:o + 5],
                                              AF.Exp, scale=-0.5))
                ln_a = sA.c

                @block.sync
                def _(sync, kp=kp, s=s):
                    _w(sync, (sP, s['h2T_end']))
                    sX.tick(sync.dma_start(out=f_yt[:, :], in_=xmid_sp.ap()[kp % 2, 0]), 16)
                    sX.tick(sync.dma_start(out=f_xr[:, :], in_=xmid_sp.ap()[kp % 2, 1]), 16)
                    _w(sync, (sX, sX.c))
                s['xm_ld'] = sX.c

                @block.vector
                def _(v, kp=kp, ln_a=ln_a, s=s):
                    _w(v, (sA, ln_a))
                    for m in range(2):
                        o = 8 * m
                        sV.tick(v.tensor_scalar_mul(f_sg[:, o + 2:o + 3],
                                                    f_sg[:, o:o + 1], -1.0))
                        _w(v, (sV, sV.c))
                        sV.tick(v.tensor_mul(f_sg[:, o + 5:o + 6], f_sg[:, o + 2:o + 3],
                                             f_sg[:, o + 3:o + 4]))
                    _w(v, (sX, s['xm_ld']))
                    for m in range(2):
                        o = 8 * m
                        stage = f_t if m == 0 else f_o3b
                        xm = f_yt if m == 0 else f_xr
                        _w(v, (sV, sV.c))
                        sV.tick(v.tensor_scalar(stage[:, 0:D], stage[:, 0:D],
                                                f_sg[:, o + 3:o + 4], f_sg[:, o + 5:o + 6],
                                                ALU.mult, ALU.add))
                        _w(v, (sV, sV.c))
                        sV.tick(v.tensor_add(stage[:, 0:D], stage[:, 0:D], xm[:, :]))
                        if m == 0:
                            s['o3v0'] = sV.c
                    s['o3v1'] = sV.c

                @block.sync
                def _(sync, kp=kp, s=s):
                    _w(sync, (sV, s['o3v0']))
                    sX.tick(sync.dma_start(out=out[kp * MT + 0], in_=f_t[:, 0:D]), 16)
                    _w(sync, (sV, s['o3v1']))
                    sX.tick(sync.dma_start(out=out[kp * MT + 1], in_=f_o3b[:, 0:D]), 16)
                    _w(sync, (sX, sX.c))
                s['out_dma'] = sX.c

            pending = []
            emit_B1(0)
            emit_B1(1)
            emit_AR(0)
            for t in range(NCHT):
                if t + 2 < NCHT:
                    emit_B1(t + 2)
                emit_B2(t)
                if pending:
                    pending.pop(0)()
                emit_B2_tail(t)
                if (t + 1) % CHG == 0:
                    kp = t // CHG
                    pending += [
                        lambda kp=kp: ((emit_W3_fin(kp - 1) if kp > 0 else None),
                                       emit_prep(kp, 0), emit_prep(kp, 1),
                                       emit_W1(kp, 0)),
                        lambda kp=kp: (emit_W1(kp, 1), emit_W2(kp, 0)),
                        lambda kp=kp: (emit_W2(kp, 1), emit_W2(kp, 2)),
                        lambda kp=kp: (emit_W2(kp, 3), emit_h2T(kp), emit_W3(kp)),
                    ]
            while pending:
                pending.pop(0)()
            emit_W3_fin(NPASS - 1)

            @block.sync
            def _(sync):
                _w(sync, (sX, sX.c))

    return nc



def _host_inputs(cfg: Cfg, x, conv_w, conv_b, xproj_w, dtproj_w, dtproj_b,
                 A_log, D_skip, W1, b1, W2, b2, W3, b3):
    B, LSEQ, D, H = cfg.B, cfg.LSEQ, cfg.D, cfg.H
    DC, G, N, K = cfg.DC, cfg.G, cfg.N, cfg.K
    NPASS, MT = cfg.NPASS, cfg.MT
    f32 = np.float32
    C = np.ascontiguousarray

    ident = np.eye(128, dtype=f32)
    xf = x.reshape(B * LSEQ, D)
    w1t = W1.T
    w2t = W2.T
    w3t = W3.T
    KT1, KTH, MB1 = cfg.KT1, cfg.KTH, cfg.MB1
    w1s = C(w1t.reshape(KT1, 128, MB1 // 2, 256).transpose(2, 1, 0, 3)
            .reshape(MB1 // 2, 128, KT1 * 256).astype(f32))
    w2s = C(w2t.reshape(KTH, 128, H // 1024, 2, 512).transpose(2, 0, 3, 1, 4)
            .reshape(KTH * (H // 512), 128, 512).astype(f32))
    w3s = C(w3t.reshape(KTH, 128, D // 1024, 2, 512).transpose(2, 0, 3, 1, 4)
            .reshape(KTH * (D // 512), 128, 512).astype(f32))

    def perm_g(a):
        t = a.reshape(G, 128, -1).transpose(1, 0, 2).reshape(128, -1)
        return C(t.astype(f32))

    in_maps = []
    for c in range(NCORES):
        sl = slice(DC * c, DC * (c + 1))
        PRc = cfg.PR
        tiles = []
        for kp in range(NPASS):
            for m in range(MT):
                r0 = PRc * kp + 128 * (c + NCORES * m)
                tiles.append(xf[r0:r0 + 128])
        xrows = np.stack(tiles, axis=0)

        d = {
            "xsl": C(x[:, :, sl]),
            "xrows": C(xrows.astype(f32)),
            "wxpT": C(xproj_w[:, sl].T.astype(f32)),
            "wdtT": C(dtproj_w[sl, :].T.astype(f32)),
            "cw": perm_g(conv_w[sl, 0, :]),
            "cb": perm_g(conv_b[sl]),
            "dtb": perm_g(dtproj_b[sl]),
            "alog": perm_g(A_log[sl]),
            "dsk": perm_g(D_skip[sl]),
            "w1f": w1s,
            "w2f": w2s,
            "w3f": w3s,
            "b1r": C(b1.reshape(cfg.MB1, 128).T.astype(f32)),
            "b2row": C(b2.reshape(1, cfg.H).astype(f32)),
            "b3r": C(b3.reshape(1, D).astype(f32)),
            "ident": ident,
        }
        in_maps.append(d)
    return in_maps


def _assemble(cfg: Cfg, results):
    B, LSEQ, D = cfg.B, cfg.LSEQ, cfg.D
    out = np.empty((B * LSEQ, D), dtype=np.float32)
    for c in range(NCORES):
        o = np.asarray(results[c]["out"])
        i = 0
        for kp in range(cfg.NPASS):
            for m in range(cfg.MT):
                r0 = cfg.PR * kp + 128 * (c + NCORES * m)
                out[r0:r0 + 128] = o[i]
                i += 1
    return out.reshape(B, LSEQ, D)


_CFG = Cfg()


def kernel(L0_attn_x, L0_attn_conv_w, L0_attn_conv_b, L0_attn_xproj_w,
           L0_attn_dtproj_w, L0_attn_dtproj_b, L0_attn_A_log, L0_attn_D_skip,
           L0_ffn_W1, L0_ffn_b1, L0_ffn_W2, L0_ffn_b2, L0_ffn_W3, L0_ffn_b3):
    cfg = _CFG
    args = [np.asarray(a, dtype=np.float32) for a in
            (L0_attn_x, L0_attn_conv_w, L0_attn_conv_b, L0_attn_xproj_w,
             L0_attn_dtproj_w, L0_attn_dtproj_b, L0_attn_A_log, L0_attn_D_skip,
             L0_ffn_W1, L0_ffn_b1, L0_ffn_W2, L0_ffn_b2, L0_ffn_W3, L0_ffn_b3)]
    in_maps = _host_inputs(cfg, *args)
    nc = build(cfg)
    res = run_bass_kernel_spmd(nc, in_maps, core_ids=list(range(NCORES)))
    return _assemble(cfg, res.results)
